# version 12
# baseline (speedup 1.0000x reference)
import numpy as np
import ml_dtypes
from contextlib import ExitStack

import concourse.bass as bass
import concourse.mybir as mybir
from concourse.bass_utils import run_bass_kernel_spmd

F32 = mybir.dt.float32
BF16 = mybir.dt.bfloat16
I16 = mybir.dt.int16
U16 = mybir.dt.uint16
AF = mybir.ActivationFunctionType
ALU = mybir.AluOpType
AXX = mybir.AxisListType.X

B, L, DM = 2, 8192, 1024
T = B * L
NCORE = 8
TOK = T // NCORE
DI = 2048
DSL = DI // NCORE
DS = 16
DCONV = 4
IFF = 3072
OSL = IFF // NCORE
QMAX = 32767.0
EPS = 1e-5
TCH = 512
NCH = T // TCH
CHB = NCH // 2
NT0 = TOK // 128


class Sem:
    __slots__ = ("h", "v", "name")

    def __init__(self, h, name):
        self.h = h
        self.v = 0
        self.name = name


class Prog:

    def __init__(self, nc, es, collect=None, edges=None):
        self.nc = nc
        self.es = es
        self.items = {"sync": [], "gpsimd": [], "vector": [], "scalar": [],
                      "tensor": []}
        self.sems = {}
        self.collect = collect
        self.edges = edges

    MERGE = {n: "misc" for n in
             ["sA", "vA", "gz", "gdiag", "t_sx", "v_sx", "v_sx2", "v_sx3",
              "t_sxb", "t4", "v4", "t5", "v5", "t6", "v6", "t7", "v8",
              "vdq", "sdqe", "tdq", "vqx", "twd", "tsm", "two", "vfm",
              "ssg", "sg1", "tfin"]}

    def sem(self, name):
        name = self.MERGE.get(name, name)
        if name not in self.sems:
            self.sems[name] = Sem(
                self.es.enter_context(self.nc.semaphore(name)), name)
        return self.sems[name]

    def add(self, eng, fn, wait=(), inc=None, incn=1):
        post = inc.v + incn if inc is not None else None
        self.items[eng].append((fn, list(wait), inc, incn, post))
        if inc is not None:
            inc.v = post
        return post

    def dma(self, eng, out, in_, stream, wait=(), rot=2):
        if not hasattr(self, "dcnt"):
            self.dcnt = {}
            self.dma_hist = {}
        i = self.dcnt.get(stream, 0)
        self.dcnt[stream] = i + 1
        j = i % rot
        sem = self.sem(f"{stream}_{j}")
        selfgate = [(sem, (i // rot) * 16)] if i >= rot else []
        self.add(eng, lambda e, o=out, i_=in_: e.dma_start(out=o, in_=i_),
                 wait=list(wait) + selfgate, inc=sem, incn=16)
        pair = (sem, (i // rot + 1) * 16)
        self.dma_hist.setdefault(stream, []).append(pair)
        return pair

    def dall(self, stream):
        out = {}
        for (sem, val) in self.dma_hist.get(stream, []):
            out[sem.name] = (sem, val)
        return list(out.values())

    @staticmethod
    def _rw(bins):
        if bins is None or not hasattr(bins, "ins"):
            return None
        inst = bins.ins
        tn = type(inst).__name__
        if ("DMA" in tn or "Dma" in tn or "Collective" in tn
                or "Sem" in tn or "Nop" in tn):
            return None

        def rng(op):
            if not hasattr(op, "memref") or op.memref is None:
                return None
            try:
                dtsz = mybir.dt.size(op.dtype)
                apl = list(op.ap)
                span = sum(abs(s) * (c - 1) for s, c in apl[1:])
                lo = op.offset * dtsz
                return (str(op.memref), lo, lo + (span + 1) * dtsz)
            except Exception:
                return None

        reads = [r for r in (rng(o) for o in inst.ins) if r is not None]
        writes = [r for r in (rng(o) for o in inst.outs) if r is not None]
        return (reads, writes)

    def emit_engine(self, ename, eng):
        watermark = {}
        edges = self.edges or {}
        pre = edges.get("pre", {})
        ainc = edges.get("inc", {})
        asem = None
        need_any_auto = any((ename, i) in ainc for i in range(len(self.items[ename])))
        if self.edges is not None and ename != "tensor" and need_any_auto:
            asem = self.sem("auto_" + ename)
        for idx, (fn, waits, inc, incn, post) in enumerate(self.items[ename]):
            allw = list(waits)
            for (sname, val) in pre.get((ename, idx), ()):
                allw.append((self.sem(sname), val))
            for (sem, val) in allw:
                if val > 0 and watermark.get(sem.name, -1) < val:
                    eng.wait_ge(sem.h, val)
                    watermark[sem.name] = val
            ins = fn(eng)
            need_auto = (ename, idx) in ainc
            if inc is not None:
                if ins is None:
                    eng.sem_inc(inc.h, incn)
                else:
                    ins.then_inc(inc.h, incn)
                assert not need_auto
            elif need_auto:
                if ins is None:
                    eng.sem_inc(asem.h, 1)
                else:
                    ins.then_inc(asem.h, 1)
            if self.collect is not None and ename != "tensor":
                rec = self._rw(ins)
                self.collect.setdefault(ename, {})[idx] = (
                    rec, inc.name if inc is not None else None, post)


def compute_edges(info):
    LINE = 32
    pre = {}
    incs = {}
    for ename, ops in info.items():
        writes_by_tensor = {}
        dep = {}
        for idx in sorted(ops):
            rec = ops[idx]
            rw = rec[0]
            if rw is None:
                continue
            reads, writes = rw
            need = -1
            for (t, lo, hi) in reads:
                for (j, lo2, hi2) in writes_by_tensor.get(t, ()):
                    if lo < hi2 and lo2 < hi:
                        need = max(need, j)
            for (t, lo, hi) in writes:
                llo, lhi = lo // LINE, (hi + LINE - 1) // LINE
                for (j, lo2, hi2) in writes_by_tensor.get(t, ()):
                    if llo < (hi2 + LINE - 1) // LINE and lo2 // LINE < lhi:
                        need = max(need, j)
            if need >= 0:
                dep[idx] = need
            for (t, lo, hi) in writes:
                writes_by_tensor.setdefault(t, []).append((idx, lo, hi))
        auto_prods = sorted({j for j in dep.values() if ops[j][1] is None})
        rank = {j: k + 1 for k, j in enumerate(auto_prods)}
        for j in auto_prods:
            incs[(ename, j)] = True
        for idx, j in dep.items():
            if ops[j][1] is not None:
                pre.setdefault((ename, idx), []).append((ops[j][1], ops[j][2]))
            else:
                pre.setdefault((ename, idx), []).append(
                    ("auto_" + ename, rank[j]))
    return {"pre": pre, "inc": incs}



class _FSB:

    def __init__(self, nc, name, shape, dt, side="right"):
        self.cm = nc.sbuf_tensor(name, shape, dt, side=side)
        self.h = self.cm.__enter__()

    def ap(self):
        return self.h.ap()

    def free(self):
        self.cm.__exit__(None, None, None)


DEBUG = False


def build_program(collect=None, edges=None):
    nc = bass.Bass("TRN2", target_bir_lowering=False,
                   detect_race_conditions=False)

    def din(name, shape, dt=F32):
        return nc.dram_tensor(name, shape, dt, kind="ExternalInput").ap()

    x_tok = din("x_tok", [TOK, DM])
    winxT = din("winxT", [DM, DSL])
    winzT = din("winzT", [DM, DSL])
    wxT = din("wxT", [DSL, 33])
    woutT = din("woutT", [DSL, DM])
    convw_in = din("convw_in", [DSL, DCONV])
    convb_in = din("convb_in", [DSL, 1])
    alog_in = din("alog_in", [DSL, DS])
    dvec_in = din("dvec_in", [DSL, 1])
    wup_rows = din("wup_rows", [OSL, DM])
    wgt_rows = din("wgt_rows", [OSL, DM])
    wdn_rows = din("wdn_rows", [128, IFF])
    ident_in = din("ident_in", [128, 128], BF16)
    identf_in = din("identf_in", [128, 128])
    onesf_in = din("onesf_in", [1, 128])
    ones16_in = din("ones16_in", [1, 16])

    y_out = nc.dram_tensor("y_out", [TOK, DM], F32, kind="ExternalOutput").ap()
    if DEBUG:
        d_xn = nc.dram_tensor("d_xn", [8, 128, TOK], F32,
                              kind="ExternalOutput").ap()
        d_lnst = nc.dram_tensor("d_lnst", [NT0, 128, 8, 8], F32,
                                kind="ExternalOutput").ap()
        d_sx = nc.dram_tensor("d_sx", [4, 128], F32,
                              kind="ExternalOutput").ap()
        d_dsp = nc.dram_tensor("d_dsp", [2, L], F32,
                               kind="ExternalOutput").ap()
        d_a32 = nc.dram_tensor("d_a32", [128, 2, DS], F32,
                               kind="ExternalOutput").ap()
        d_scal = nc.dram_tensor("d_scal", [128, 16, 8], F32,
                                kind="ExternalOutput").ap()
        d_kq = nc.dram_tensor("d_kq", [8, 128, TOK], BF16,
                              kind="ExternalOutput").ap()
        d_xi = nc.dram_tensor("d_xi", [2, 128, T], BF16,
                              kind="ExternalOutput").ap()
        d_ssm = nc.dram_tensor("d_ssm", [2, 33, L], F32,
                               kind="ExternalOutput").ap()
        d_rsin = nc.dram_tensor("d_rsin", [8, 8, 128, TOK], BF16,
                                kind="ExternalOutput").ap()
        d_rsout = nc.dram_tensor("d_rsout", [2, 8, 128, TOK], BF16,
                                 kind="ExternalOutput").ap()
        d_up = nc.dram_tensor("d_up", [24, 128, TOK], F32,
                              kind="ExternalOutput").ap()
        d_agu = nc.dram_tensor("d_agu", [8, 8, 128, OSL], BF16,
                               kind="ExternalOutput").ap()
        dbg = SS("dbg") if False else None

    def dint(name, shape, dt=F32):
        return nc.dram_tensor(name, shape, dt).ap()

    ag_xn_in = dint("ag_xn_in", [DM, TOK], BF16)
    ag_xn_out = dint("ag_xn_out", [NCORE, DM, TOK], BF16)
    agu_in = dint("agu_in", [8, 128, OSL], BF16)
    agu_out = dint("agu_out", [NCORE, 8, 128, OSL], BF16)
    agg_in = dint("agg_in", [8, 128, OSL], BF16)
    agg_out = dint("agg_out", [NCORE, 8, 128, OSL], BF16)
    agd_in = dint("agd_in", [24, 128, 128], BF16)
    agd_out = dint("agd_out", [NCORE, 24, 128, 128], BF16)
    ar_ssm_in = [dint(f"ar_ssm_in{b}", [33, L]) for b in range(2)]
    ar_ssm_out = [dint(f"ar_ssm_out{b}", [33, L]) for b in range(2)]
    dsp_dram = [dint(f"dsp{b}", [1, L]) for b in range(2)]
    ars_in = [dint(f"ars_in{i}", [1, 8]) for i in range(4)]
    ars_out = [dint(f"ars_out{i}", [1, 8]) for i in range(4)]
    rs_in = [dint(f"rs_in{b}", [NCORE, 8, 128, TOK], BF16) for b in range(2)]
    rs_out = [dint(f"rs_out{b}", [8, 128, TOK], BF16) for b in range(2)]
    kq_dram = dint("kq_dram", [8, 128, TOK], BF16)
    up_spill = dint("up_spill", [24, 128, TOK])
    gt_spill = dint("gt_spill", [24, 128, TOK])

    es = ExitStack()
    P = Prog(nc, es, collect=collect, edges=edges)
    SS = P.sem
    add = P.add
    dma = P.dma

    def sbp(name, shape, dt):
        return es.enter_context(nc.sbuf_tensor(name, shape, dt)).ap()

    winxTb = sbp("winxTb", [128, 8, DSL], BF16)
    winzTb = sbp("winzTb", [128, 8, DSL], BF16)
    wxTb = sbp("wxTb", [128, 2, 33], BF16)
    woutTb = sbp("woutTb", [128, 2, DM], BF16)
    convdiag = sbp("convdiag", [128, 2, DCONV, 128], BF16)
    ddiag = sbp("ddiag", [128, 2, 128], BF16)
    identb = sbp("identb", [128, 128], BF16)
    identf = sbp("identf", [128, 128], F32)
    o1f = sbp("o1f", [1, 128], F32)
    o116 = sbp("o116", [1, 16], F32)
    a32 = sbp("a32", [128, 2, DS], F32)
    convw = sbp("convw", [128, 2, DCONV], F32)
    convb = sbp("convb", [128, 2, 1], F32)
    alog = sbp("alog", [128, 2, DS], F32)
    dvec = sbp("dvec", [128, 2, 1], F32)
    scal = sbp("scal", [128, 16, 8], F32)
    tiny = sbp("tiny", [1, 8, 8], F32)
    zerosb = sbp("zerosb", [128, TCH], BF16)

    pA_h = es.enter_context(nc.psum_tensor("pA", [128, 2, TCH], F32))
    pB_h = es.enter_context(nc.psum_tensor("pB", [128, 2, TCH], F32))
    pC_h = es.enter_context(nc.psum_tensor("pC", [128, 2, TCH], F32))
    pD_h = es.enter_context(nc.psum_tensor("pD", [128, TCH], F32))
    pE_h = es.enter_context(nc.psum_tensor("pE", [128, TCH], F32))
    pA, pB, pC, pD, pE = (pA_h.ap(), pB_h.ap(), pC_h.ap(), pD_h.ap(),
                          pE_h.ap())
    pCb = pC_h.bitcast(BF16).ap()

    for dst, src in [(identb, ident_in), (o1f, onesf_in),
                     (o116, ones16_in),
                     (identf, identf_in),
                     (convw[:, 0, :], convw_in[0:128, :]),
                     (convw[:, 1, :], convw_in[128:256, :]),
                     (convb[:, 0, :], convb_in[0:128, :]),
                     (convb[:, 1, :], convb_in[128:256, :]),
                     (alog[:, 0, :], alog_in[0:128, :]),
                     (alog[:, 1, :], alog_in[128:256, :]),
                     (dvec[:, 0, :], dvec_in[0:128, :]),
                     (dvec[:, 1, :], dvec_in[128:256, :])]:
        dma("sync", dst, src, "ldw")
    LDW_DONE = P.dall("ldw")


    gdiag = SS("gdiag")
    for j in range(2):
        for k in range(DCONV):
            add("gpsimd",
                lambda e, j=j, k=k: e.affine_select(
                    out=convdiag[:, j, k, :],
                    in_=convw[:, j, k:k + 1].broadcast_to((128, 128)),
                    pattern=[[1, 128]], compare_op=ALU.is_equal, fill=0.0,
                    base=0, channel_multiplier=-1),
                wait=LDW_DONE, inc=gdiag)
        add("gpsimd",
            lambda e, j=j: e.affine_select(
                out=ddiag[:, j, :],
                in_=dvec[:, j, 0:1].broadcast_to((128, 128)),
                pattern=[[1, 128]], compare_op=ALU.is_equal, fill=0.0,
                base=0, channel_multiplier=-1),
            wait=LDW_DONE, inc=gdiag)
    gdiag_done = gdiag.v

    sA = SS("sA")
    sA_t = add("scalar",
               lambda e: e.activation(a32[:, :, :], alog[:, :, :], AF.Exp,
                                      bias=0.0, scale=1.0),
               wait=LDW_DONE, inc=sA)
    vA = SS("vA")
    add("vector", lambda e: e.tensor_scalar_mul(a32[:, :, :], a32[:, :, :],
                                                -1.0),
        wait=[(sA, sA_t)], inc=vA)
    vA_done = vA.v

    gz = SS("gz")
    gz_t = add("gpsimd", lambda e: e.memset(zerosb[:, :], 0.0), inc=gz)

    xt_h = _FSB(nc, "xt", [128, 2, DM], F32)
    sqd_h = _FSB(nc, "sqd", [128, DM], F32)
    lnst_h = _FSB(nc, "lnst", [128, 8, 8], F32)
    xn32_h = _FSB(nc, "xn32", [128, 2, DM], F32)
    xnT32_h = _FSB(nc, "xnT32", [128, 8, TOK], F32)
    xnbst_h = _FSB(nc, "xnbst", [128, 8, 128], BF16)
    sxacc_h = _FSB(nc, "sxacc", [128, 2, 8], F32)
    kqi_h = _FSB(nc, "kqi", [128, TOK], I16)
    kqb_h = _FSB(nc, "kqb", [128, 2, TOK], BF16)
    xt = xt_h.ap()
    sqd = sqd_h.ap()
    lnst = lnst_h.ap()
    xn32 = xn32_h.ap()
    xnT32 = xnT32_h.ap()
    xnbst = xnbst_h.ap()
    sxacc = sxacc_h.ap()
    kqi = kqi_h.ap()
    kqb = kqb_h.ap()

    vln = SS("vln")
    sln = SS("sln")
    gxn_pairs = []
    tln = SS("tln")
    sevac = SS("sevac")
    vevac = SS("vevac")
    cc = SS("cc")

    VPT = 11
    for tt in range(NT0):
        bsl = tt % 2
        pldx = dma("sync", xt[:, bsl, :], x_tok[tt * 128:(tt + 1) * 128, :],
                   "ldx", wait=[(vln, max(0, (tt - 2) * VPT + 9)),
                                (sln, max(0, (tt - 2) * 2 + 1))], rot=2)
        add("scalar",
            lambda e, b=bsl: e.activation(sqd[:, :], xt[:, b, :], AF.Square),
            wait=[pldx, (vln, max(0, (tt - 1) * VPT + 5))] +
            (P.dall("dbg") if DEBUG else []), inc=sln)
        sln_sq = sln.v
        add("vector",
            lambda e: e.reduce_sum(lnst[:, 1, 0:1], sqd[:, :], AXX),
            wait=[(sln, sln_sq)], inc=vln)
        add("vector",
            lambda e, b=bsl: e.reduce_sum(lnst[:, 0, 0:1], xt[:, b, :], AXX),
            wait=[pldx], inc=vln)
        add("vector", lambda e: e.tensor_scalar_mul(lnst[:, 0, 0:1], lnst[:, 0, 0:1],
                                                    1.0 / DM), inc=vln)
        add("vector", lambda e: e.tensor_mul(lnst[:, 5, 0:1], lnst[:, 0, 0:1],
                                             lnst[:, 0, 0:1]), inc=vln)
        add("vector",
            lambda e: e.tensor_scalar(lnst[:, 2, 0:1], lnst[:, 1, 0:1], 1.0 / DM,
                                      None, ALU.mult),
            wait=[(sln, sln_sq)], inc=vln)
        add("vector",
            lambda e: e.scalar_tensor_tensor(lnst[:, 2, 0:1], lnst[:, 5, 0:1], -1.0,
                                             lnst[:, 2, 0:1], ALU.mult, ALU.add),
            inc=vln)
        add("vector",
            lambda e: e.tensor_scalar(lnst[:, 2, 0:1], lnst[:, 2, 0:1], 1.0, EPS,
                                      ALU.mult, ALU.add), inc=vln)
        add("vector", lambda e: e.reciprocal(lnst[:, 3, 0:1], lnst[:, 2, 0:1]),
            inc=vln)
        add("scalar", lambda e: e.activation(lnst[:, 4, 0:1], lnst[:, 3, 0:1],
                                             AF.Sqrt, bias=0.0, scale=1.0),
            wait=[(vln, tt * VPT + 8)], inc=sln)
        sln_r = sln.v
        add("vector",
            lambda e, b=bsl: e.tensor_scalar(xn32[:, b, :], xt[:, b, :],
                                             lnst[:, 0, 0:1], lnst[:, 4, 0:1],
                                             ALU.subtract, ALU.mult),
            wait=[(sln, sln_r), (tln, max(0, (tt - 2) * 8 + 8))], inc=vln)
        vln_xn = vln.v
        add("vector",
            lambda e, b=bsl: e.reduce_max(sxacc[:, 1, 0:1], xn32[:, b, :], AXX,
                                          apply_absolute_value=True), inc=vln)
        if tt == 0:
            add("vector", lambda e: e.tensor_copy(sxacc[:, 0, 0:1],
                                                  sxacc[:, 1, 0:1]), inc=vln)
        else:
            add("vector", lambda e: e.tensor_max(sxacc[:, 0, 0:1], sxacc[:, 0, 0:1],
                                                 sxacc[:, 1, 0:1]), inc=vln)
        assert vln.v == (tt + 1) * VPT
        if False:
            dma("gpsimd", d_lnst[tt], lnst[:, :, :], "dbg",
                wait=[(vln, vln.v), (sln, sln.v)])
        for dj in range(8):
            dst = pA if dj < 4 else pB
            sub = dj % 4
            add("tensor",
                lambda e, b=bsl, dj=dj, dst=dst, sub=sub: e.transpose(
                    dst[:, sub // 2, (sub % 2) * 128:(sub % 2 + 1) * 128],
                    xn32[:, b, dj * 128:(dj + 1) * 128], identf[:, :]),
                wait=[(vln, vln_xn), (sevac, max(0, tt * 2))] + LDW_DONE,
                inc=tln)
        tln_t = tln.v
        for half in range(2):
            src = pA if half == 0 else pB
            add("scalar",
                lambda e, tt=tt, h=half, src=src: e.activation(
                    xnT32[:, h * 4:(h + 1) * 4,
                          tt * 128:(tt + 1) * 128].rearrange(
                              "p (a c) t -> p a c t", a=2),
                    src[:, :, 0:256].rearrange("p a (c t) -> p a c t", t=128),
                    AF.Copy, bias=0.0, scale=1.0),
                wait=[(tln, tln_t), (vevac, max(0, tt - 1))], inc=sevac)
        sevac_t = sevac.v
        add("vector",
            lambda e, tt=tt: e.tensor_copy(
                xnbst[:, :, :], xnT32[:, :, tt * 128:(tt + 1) * 128]),
            wait=[(sevac, sevac_t)] + (gxn_pairs[tt - 1] if tt > 0 else []),
            inc=vevac)
        vevac_t = vevac.v
        prs = []
        for a8 in range(8):
            prs.append(dma(
                "gpsimd",
                ag_xn_in[a8 * 128:(a8 + 1) * 128, tt * 128:(tt + 1) * 128],
                xnbst[:, a8, :], "gxn", wait=[(vevac, vevac_t)], rot=4))
        gxn_pairs.append(prs)
    vln_all = vln.v
    sevac_all = sevac.v
    tln_all = tln.v


    t_sx = SS("t_sx")
    add("tensor", lambda e: e.transpose(pD[0:1, 0:128], sxacc[:, 0, 0:1],
                                        identf[:, :]),
        wait=[(vln, vln_all)], inc=t_sx)
    v_sx = SS("v_sx")
    add("vector", lambda e: e.memset(tiny[:, :, :], 0.0), inc=v_sx)
    add("vector", lambda e: e.reduce_max(tiny[0:1, 0, 0:1], pD[0:1, 0:128], AXX),
        wait=[(t_sx, t_sx.v)], inc=v_sx)
    if False:
        dma("gpsimd", d_sx[0, 0:8], tiny[0:1, 0, 0:8],
            "dbg", wait=[(v_sx, v_sx.v)])
        dma("gpsimd", d_sx[3, :], sxacc[:, 0, 0:1],
            "dbg", wait=[(vln, vln_all)])
    parsx = dma("gpsimd", ars_in[0][:, :], tiny[0:1, 0, 0:8], "gars",
                wait=[(v_sx, v_sx.v)])
    add("gpsimd",
        lambda e: e.collective_compute(
            "AllReduce", ALU.max, replica_groups=[list(range(NCORE))],
            ins=[ars_in[0]], outs=[ars_out[0]]),
        wait=[parsx], inc=cc)
    cc_sx = cc.v
    pldsx = dma("scalar", tiny[0:1, 4, 0:8], ars_out[0][:, :], "lars",
                wait=[(cc, cc_sx)])
    v_sx2 = SS("v_sx2")
    if False:
        dma("gpsimd", d_sx[1, 0:8], tiny[0:1, 4, 0:8],
            "dbg", wait=[pldsx])
    add("vector",
        lambda e: e.tensor_scalar(tiny[0:1, 0, 0:1], tiny[0:1, 4, 0:1], 1.0 / QMAX,
                                  1e-12, ALU.mult, ALU.max),
        wait=[pldsx] + P.dall("dbg"), inc=v_sx2)
    t_sxb = SS("t_sxb")
    add("tensor",
        lambda e: e.matmul(pD[:, 0:1], o1f[0:1, :], tiny[0:1, 0, 0:1],
                           start=True, stop=True),
        wait=[(v_sx2, v_sx2.v), (v_sx, v_sx.v)] + LDW_DONE, inc=t_sxb)
    v_sx3 = SS("v_sx3")
    add("vector", lambda e: e.tensor_copy(scal[:, 0, 0:1], pD[:, 0:1]),
        wait=[(t_sxb, t_sxb.v)], inc=v_sx3)
    add("vector", lambda e: e.reciprocal(scal[:, 1, 0:1], scal[:, 0, 0:1]),
        inc=v_sx3)
    v_sx_done = v_sx3.v
    add("gpsimd",
        lambda e: e.collective_compute(
            "AllGather", ALU.bypass, replica_groups=[list(range(NCORE))],
            ins=[ag_xn_in], outs=[ag_xn_out]),
        wait=P.dall("gxn"), inc=cc)
    cc_agxn = cc.v
    for k in range(8):
        dma("gpsimd", winxTb[:, k, :], winxT[k * 128:(k + 1) * 128, :], "ldg")
        dma("gpsimd", winzTb[:, k, :], winzT[k * 128:(k + 1) * 128, :], "ldg")
    for j in range(2):
        dma("gpsimd", wxTb[:, j, :], wxT[j * 128:(j + 1) * 128, :], "ldg")
        dma("gpsimd", woutTb[:, j, :], woutT[j * 128:(j + 1) * 128, :], "ldg")
    LDG_DONE = P.dall("ldg")

    if False:
        dma("gpsimd", d_sx[2, 0:64],
            scal[0:1, 0:8, 0:8], "dbg",
            wait=[(v_sx3, v_sx_done)])
    vqx = SS("vqx")
    gkq_pairs = []
    for k in range(8):
        add("vector",
            lambda e, k=k: e.tensor_scalar(kqi[:, :], xnT32[:, k, :],
                                           scal[:, 1, 0:1], None, ALU.mult),
            wait=[(v_sx3, v_sx_done), (sevac, sevac_all)], inc=vqx)
        add("vector",
            lambda e, k=k: e.tensor_copy(kqb[:, k % 2, :], kqi[:, :]),
            wait=([gkq_pairs[k - 2]] if k >= 2 else []), inc=vqx)
        gkq_pairs.append(dma("gpsimd", kq_dram[k], kqb[:, k % 2, :], "gkq",
                             wait=[(vqx, vqx.v)], rot=2))
    vqx_done = vqx.v
    if DEBUG:
        for k in range(8):
            dma("gpsimd", d_xn[k, :, :], xnT32[:, k, :], "dbg",
                wait=[(sevac, sevac_all)], rot=2)
            dma("gpsimd", d_kq[k, :, :], kq_dram[k, :, :], "dbg",
                wait=P.dall("gkq"), rot=2)

    dqrows_h = _FSB(nc, "dqrows", [128, 3 * DM], F32)
    dqbm_h = _FSB(nc, "dqbm", [128, 3, DM // 32], F32)
    dqs_h = _FSB(nc, "dqs", [128, 3, DM // 32], F32)
    dqrs_h = _FSB(nc, "dqrs", [128, 3, DM // 32], F32)
    dqq_h = _FSB(nc, "dqq", [128, 3 * DM], I16)
    dqb_h = _FSB(nc, "dqb", [128, 3 * DM], BF16)
    dnbm_h = _FSB(nc, "dnbm", [128, IFF // 32], F32)
    dnsv_h = _FSB(nc, "dnsv", [128, IFF // 32], F32)
    dnrv_h = _FSB(nc, "dnrv", [128, IFF // 32], F32)
    tpst_h = _FSB(nc, "tpst", [128, 2, 128], BF16)
    dqrows = dqrows_h.ap()
    dqbm = dqbm_h.ap()
    dqs = dqs_h.ap()
    dqrs = dqrs_h.ap()
    dqq = dqq_h.ap()
    dqb = dqb_h.ap()
    dnbm = dnbm_h.ap()
    dnsv = dnsv_h.ap()
    dnrv = dnrv_h.ap()
    tpst = tpst_h.ap()

    vdq = SS("vdq")
    tdq = SS("tdq")
    sdqe = SS("sdqe")

    def emit_dequant(which):
        if which == "down":
            nrt, kdim = 1, IFF
            bmall, sall, rall = dnbm[:, :], dnsv[:, :], dnrv[:, :]
            agdst = agd_in
        else:
            nrt, kdim = 3, DM
            bmall, sall, rall = dqbm[:, :, :], dqs[:, :, :], dqrs[:, :, :]
            agdst = agu_in if which == "up" else agg_in
        nb32 = kdim // 32
        lddq_w = P.dall("lddq")
        tdq_prev = tdq.v
        sdqe_prev = sdqe.v
        for r in range(nrt):
            rin = dqrows[:, r * kdim:(r + 1) * kdim]
            bsl = dqbm[:, r, :] if which != "down" else dnbm[:, :]
            add("vector",
                lambda e, rin=rin, bsl=bsl: e.tensor_reduce(
                    bsl, rin.rearrange("p (a b) -> p a b", b=32), AXX, ALU.max,
                    apply_absolute_value=True),
                wait=lddq_w, inc=vdq)
        add("vector",
            lambda e, sall=sall, bmall=bmall: e.tensor_scalar(
                sall, bmall, 1.0 / 7.0, 1e-12, ALU.mult, ALU.max), inc=vdq)
        add("vector", lambda e, rall=rall, sall=sall: e.reciprocal(rall, sall),
            inc=vdq)
        for r in range(nrt):
            rin = dqrows[:, r * kdim:(r + 1) * kdim]
            qsl = dqq[:, r * kdim:(r + 1) * kdim]
            bsl2 = dqb[:, r * kdim:(r + 1) * kdim]
            rsl = dqrs[:, r, :] if which != "down" else dnrv[:, :]
            ssl = dqs[:, r, :] if which != "down" else dnsv[:, :]
            add("vector",
                lambda e, qsl=qsl, rin=rin, rsl=rsl, nb32=nb32: e.tensor_tensor(
                    qsl.rearrange("p (a b) -> p a b", b=32),
                    rin.rearrange("p (a b) -> p a b", b=32),
                    rsl.rearrange("p (a b) -> p a b", b=1).broadcast_to((128, nb32, 32)),
                    ALU.mult),
                wait=[(tdq, tdq_prev), (sdqe, sdqe_prev)], inc=vdq)
            add("vector",
                lambda e, qsl=qsl: e.tensor_scalar(qsl, qsl, -8.0, 7.0,
                                                   ALU.max, ALU.min), inc=vdq)
            add("vector",
                lambda e, bsl2=bsl2, qsl=qsl, ssl=ssl, nb32=nb32:
                e.tensor_tensor(
                    bsl2.rearrange("p (a b) -> p a b", b=32),
                    qsl.rearrange("p (a b) -> p a b", b=32),
                    ssl.rearrange("p (a b) -> p a b", b=1).broadcast_to((128, nb32, 32)),
                    ALU.mult), inc=vdq)
        vdq_t = vdq.v
        for r in range(nrt):
            for ckt in range(kdim // 128):
                slot = (r * (kdim // 128) + ckt) % 2
                add("tensor",
                    lambda e, r=r, ckt=ckt, kdim=kdim: e.transpose(
                        pCb[:, 0, 0:128],
                        dqb[:, r * kdim + ckt * 128:r * kdim + (ckt + 1) * 128],
                        identb[:, :]),
                    wait=[(vdq, vdq_t), (sdqe, sdqe.v), (tln, tln_all)],
                    inc=tdq)
                gh = P.dma_hist.get("gwc", []) if hasattr(P, "dma_hist") else []
                add("scalar",
                    lambda e, slot=slot: e.activation(
                        tpst[:, slot, :], pCb[:, 0, 0:128], AF.Copy, bias=0.0,
                        scale=1.0),
                    wait=[(tdq, tdq.v)] + ([gh[-2]] if len(gh) >= 2 else []),
                    inc=sdqe)
                dma("gpsimd", agdst[ckt, :, r * 128:(r + 1) * 128],
                    tpst[:, slot, :], "gwc", wait=[(sdqe, sdqe.v)])

    for r in range(3):
        dma("sync", dqrows[:, r * DM:(r + 1) * DM],
            wup_rows[r * 128:(r + 1) * 128, :], "lddq")
    emit_dequant("up")
    for r in range(3):
        dma("sync", dqrows[:, r * DM:(r + 1) * DM],
            wgt_rows[r * 128:(r + 1) * 128, :], "lddq", wait=[(vdq, vdq.v)])
    emit_dequant("gate")
    dma("sync", dqrows[:, :], wdn_rows[:, :], "lddq", wait=[(vdq, vdq.v)])
    emit_dequant("down")

    for (ain, aout) in [(agu_in, agu_out), (agg_in, agg_out),
                        (agd_in, agd_out)]:
        add("gpsimd",
            lambda e, ain=ain, aout=aout: e.collective_compute(
                "AllGather", ALU.bypass, replica_groups=[list(range(NCORE))],
                ins=[ain], outs=[aout]),
            wait=P.dall("gwc"), inc=cc)
    cc_agw = cc.v
    vdq_all = vdq.v
    sdqe_all = sdqe.v
    tdq_all = tdq.v

    for c in range(4, 8):
        for nt in range(8):
            for q in range(4):
                dma("sync", rs_in[0][c, nt, :, q * TCH:(q + 1) * TCH],
                    zerosb[:, :], "zrs", wait=[(gz, gz_t)], rot=2)
    for c in range(0, 4):
        for nt in range(8):
            for q in range(4):
                dma("sync", rs_in[1][c, nt, :, q * TCH:(q + 1) * TCH],
                    zerosb[:, :], "zrs", wait=[(gz, gz_t)], rot=2)
    ZRS_DONE = P.dall("zrs")
    tpst_h.free()
    dnrv_h.free()
    dnsv_h.free()
    dnbm_h.free()
    dqb_h.free()
    dqq_h.free()
    dqrs_h.free()
    dqs_h.free()
    dqbm_h.free()
    dqrows_h.free()
    kqb_h.free()
    kqi_h.free()
    sxacc_h.free()
    xnbst_h.free()
    xnT32_h.free()
    xn32_h.free()
    lnst_h.free()
    sqd_h.free()
    xt_h.free()

    x_inner_h = _FSB(nc, "x_inner", [128, 2, T], BF16)
    rhsxn_h = _FSB(nc, "rhsxn", [128, 2, 8, TCH], BF16)
    xzpre_h = _FSB(nc, "xzpre", [128, 2, 2, TCH + 3], BF16)
    ssmst_h = _FSB(nc, "ssmst", [33, 2, TCH], F32)
    x_inner = x_inner_h.ap()
    rhsxn = rhsxn_h.ap()
    xzpre = xzpre_h.ap()
    ssmst = ssmst_h.ap()

    REUSE_W = (P.dall("dbg") if DEBUG else []) + [
        (vqx, vqx_done), (vdq, vdq_all), (sdqe, sdqe_all),
        (tdq, tdq_all), (vln, vln_all),
        (sln, P.sems["sln"].v), (tln, tln_all),
        (vevac, P.sems["vevac"].v)] + P.dall("gwc") + P.dall("gxn") + P.dall("gkq")

    tfr = SS("tfr")
    sfe = SS("sfe")
    vha = SS("vha")
    tcv = SS("tcv")
    ssi = SS("ssi")
    tsm = SS("tsm")
    vse = SS("vse")
    gsm_pairs = []

    for ch in range(NCH):
        par = ch % 2
        rch = ch // 4
        coff = (ch % 4) * TCH
        bat = ch // CHB
        chb = ch % CHB
        pldr = dma("sync", rhsxn[:, par, :, :],
                   ag_xn_out[rch].rearrange("(a p) t -> p a t", p=128)[
                       :, :, coff:coff + TCH],
                   "ldr", wait=[(cc, cc_agxn),
                                (tfr, max(0, (ch - 1) * 16))] + REUSE_W,
                   rot=2)
        for ntl in range(2):
            for k in range(8):
                add("tensor",
                    lambda e, ntl=ntl, k=k, par=par: e.matmul(
                        pA[:, ntl, :],
                        winxTb[:, k, ntl * 128:(ntl + 1) * 128],
                        rhsxn[:, par, k, :], start=(k == 0), stop=(k == 7)),
                    wait=[pldr] + LDG_DONE +
                    [(sfe, max(0, ch * 2)), (sevac, sevac_all),
                     (t_sxb, t_sxb.v), (tdq, tdq_all)],
                    inc=tfr)
        tfr_t = tfr.v
        if chb == 0:
            add("vector",
                lambda e, par=par: e.memset(xzpre[:, par, :, 0:3], 0.0),
                wait=[(ssi, max(0, (ch - 2) * 2 + 2))] + REUSE_W, inc=vha)
        else:
            add("vector",
                lambda e, par=par: e.tensor_copy(
                    xzpre[:, par, :, 0:3], xzpre[:, 1 - par, :, TCH:TCH + 3]),
                wait=[(sfe, max(0, ch * 2)),
                      (ssi, max(0, (ch - 2) * 2 + 2))], inc=vha)
        vha_t = vha.v
        for j in range(2):
            add("scalar",
                lambda e, j=j, par=par: e.activation(
                    xzpre[:, par, j, 3:TCH + 3], pA[:, j, :], AF.Copy,
                    bias=0.0, scale=1.0),
                wait=[(tfr, tfr_t), (vha, vha_t),
                      (tcv, max(0, (ch - 1) * 8))], inc=sfe)
        sfe_t = sfe.v
        for j in range(2):
            for k in range(DCONV):
                add("tensor",
                    lambda e, j=j, k=k, par=par: e.matmul(
                        pB[:, j, :], convdiag[:, j, k, :],
                        xzpre[:, par, j, k:k + TCH],
                        start=(k == 0), stop=(k == DCONV - 1)),
                    wait=[(sfe, sfe_t), (gdiag, gdiag_done),
                          (ssi, max(0, ch * 2))],
                    inc=tcv)
        tcv_t = tcv.v
        for j in range(2):
            add("scalar",
                lambda e, j=j, ch=ch: e.activation(
                    x_inner[:, j, ch * TCH:(ch + 1) * TCH], pB[:, j, :],
                    AF.Silu, bias=convb[:, j, 0:1], scale=1.0),
                wait=[(tcv, tcv_t)], inc=ssi)
        ssi_t = ssi.v
        for j in range(2):
            add("tensor",
                lambda e, j=j, ch=ch: e.matmul(
                    pE[0:33, :], wxTb[:, j, :],
                    x_inner[:, j, ch * TCH:(ch + 1) * TCH],
                    start=(j == 0), stop=(j == 1)),
                wait=[(ssi, ssi_t), (vse, max(0, ch))],
                inc=tsm)
        tsm_t = tsm.v
        add("vector",
            lambda e, par=par: e.tensor_copy(ssmst[:, par, :], pE[0:33, :]),
            wait=[(tsm, tsm_t)] +
            ([gsm_pairs[ch - 2]] if ch >= 2 else []), inc=vse)
        vse_t = vse.v
        gsm_pairs.append(dma(
            "gpsimd", ar_ssm_in[bat][:, chb * TCH:(chb + 1) * TCH],
            ssmst[:, par, :], "gsm", wait=[(vse, vse_t)], rot=2))
        if chb == CHB - 1:
            add("gpsimd",
                lambda e, bat=bat: e.collective_compute(
                    "AllReduce", ALU.add, replica_groups=[list(range(NCORE))],
                    ins=[ar_ssm_in[bat]], outs=[ar_ssm_out[bat]]),
                wait=gsm_pairs[bat * CHB:(bat + 1) * CHB], inc=cc)
    cc_ssm = {0: cc_agw + 1, 1: cc_agw + 2}
    assert cc.v == cc_ssm[1]
    P1T = {"tfr": tfr.v, "sfe": sfe.v, "tcv": tcv.v, "ssi": ssi.v,
           "tsm": tsm.v, "vse": vse.v}
    REUSE_P1 = [(tcv, P1T["tcv"]), (vse, P1T["vse"]),
                (vha, vha.v)] + P.dall("gsm")

    ssmst_h.free()
    xzpre_h.free()
    dspr_h = _FSB(nc, "dspr", [1, 1, 512], F32)
    bcstB_h = _FSB(nc, "bcstB", [16, TCH], F32)
    bcstC_h = _FSB(nc, "bcstC", [16, TCH], F32)
    bcstD_h = _FSB(nc, "bcstD", [16, TCH], F32)
    wch_h = _FSB(nc, "wch", [16, 2, TCH], BF16)
    cch_h = _FSB(nc, "cch", [16, 2, TCH], BF16)
    dAb_h = _FSB(nc, "dAb", [128, 2, 2, TCH], F32)
    ubuf_h = _FSB(nc, "ubuf", [128, 2, 2, TCH], BF16)
    hbuf_h = _FSB(nc, "hbuf", [128, 2, 2, TCH], F32)
    vbuf_h = _FSB(nc, "vbuf", [128, 2, 2, TCH], BF16)
    state_h = _FSB(nc, "state", [128, 2, 2, DS], F32)
    gsil_h = _FSB(nc, "gsil", [128, 2, TCH], BF16)
    abuf_h = _FSB(nc, "abuf", [128, 2, TCH], BF16)
    attnst_h = _FSB(nc, "attnst", [128, 4, TCH], BF16)
    bigB_h = _FSB(nc, "bigB", [128, 2, DS, TCH], BF16)
    bigC_h = _FSB(nc, "bigC", [128, 2, DS, TCH], BF16)
    bcw_dram = dint("bcw_dram", [2, DS, TCH], BF16)
    bcc_dram = dint("bcc_dram", [2, DS, TCH], BF16)
    dspr = dspr_h.ap()
    bcstB = bcstB_h.ap()
    bcstC = bcstC_h.ap()
    bcstD = bcstD_h.ap()
    wch = wch_h.ap()
    cch = cch_h.ap()
    dAb = dAb_h.ap()
    ubuf = ubuf_h.ap()
    hbuf = hbuf_h.ap()
    vbuf = vbuf_h.ap()
    state = state_h.ap()
    gsil = gsil_h.ap()
    abuf = abuf_h.ap()
    attnst = attnst_h.ap()
    bigB = bigB_h.ap()
    bigC = bigC_h.ap()

    vwd = SS("vwd")
    td1 = SS("td1")
    sda = SS("sda")
    vu = SS("vu")
    vsc = SS("vsc")
    vst = SS("vst")
    vv = SS("vv")
    tsr = SS("tsr")
    tz = SS("tz")
    ssz = SS("ssz")
    va = SS("va")
    two = SS("two")
    swo = SS("swo")
    grs_pairs = []
    stw_pairs = []
    stc_pairs = []
    bigB_pairs = []
    bigC_pairs = []

    P1_DONE = [(sfe, P1T["sfe"]), (ssi, P1T["ssi"]), (vse, P1T["vse"]),
               (tsm, P1T["tsm"])]

    ssp = SS("ssp")
    dsp_pairs = [[], []]
    NSEG = L // 512
    for bat in range(2):
        for sg in range(NSEG):
            buf = 0
            iglob = bat * NSEG + sg
            allprev = dsp_pairs[0] + dsp_pairs[1] if bat else dsp_pairs[0]
            pin = dma("sync", dspr[0:1, buf, :],
                      ar_ssm_out[bat][32:33, sg * 512:(sg + 1) * 512],
                      "ldbc", wait=[(cc, cc_ssm[bat]),
                                    (ssp, max(0, iglob * 2))] +
                      REUSE_P1 +
                      ([allprev[iglob - 1]] if iglob >= 1 else []),
                      rot=3)
            add("scalar",
                lambda e, buf=buf: e.activation(dspr[0:1, buf, :],
                                                dspr[0:1, buf, :], AF.Exp,
                                                bias=0.0, scale=1.0),
                wait=[pin],
                inc=ssp)
            add("scalar",
                lambda e, buf=buf: e.activation(dspr[0:1, buf, :],
                                                dspr[0:1, buf, :], AF.Ln,
                                                bias=1.0, scale=1.0),
                inc=ssp)
            dsp_pairs[bat].append(dma(
                "gpsimd", dsp_dram[bat][:, sg * 512:(sg + 1) * 512],
                dspr[0:1, buf, :], "gsm", wait=[(ssp, ssp.v)], rot=2))

    if DEBUG:
        for bat in range(2):
            dma("gpsimd", d_dsp[bat], dsp_dram[bat][0, :], "dbg",
                wait=[dsp_pairs[bat][-1], dsp_pairs[bat][-2]] +
                [p for p in dsp_pairs[bat]])
        dma("gpsimd", d_a32[:, :, :], a32[:, :, :], "dbg",
            wait=[(vA, vA_done)])
    prol = {}

    def emit_prologue_a(ch):
        par = ch % 2
        bat = ch // CHB
        chb = ch % CHB
        wbc = [(cc, cc_ssm[bat]), (vwd, max(0, ch * 2)),
               (td1, max(0, ch))] + REUSE_P1
        pbcB = dma("sync", bcstB[:, :],
                   ar_ssm_out[bat][0:16, chb * TCH:(chb + 1) * TCH], "ldbc",
                   wait=wbc, rot=3)
        pbcC = dma("sync", bcstC[:, :],
                   ar_ssm_out[bat][16:32, chb * TCH:(chb + 1) * TCH], "ldbc",
                   wait=wbc, rot=3)
        pbcD = dma("sync", bcstD[:, :],
                   dsp_dram[bat][:, chb * TCH:(chb + 1) * TCH].broadcast_to(
                       (16, TCH)), "ldbc",
                   wait=wbc + [dsp_pairs[bat][(chb * TCH) // 512]], rot=3)
        prol[ch] = {"pbcB": pbcB, "pbcC": pbcC, "pbcD": pbcD}

    def emit_prologue_b(ch):
        par = ch % 2
        d = prol[ch]
        add("vector",
            lambda e, par=par: e.tensor_tensor(wch[:, par, :],
                                               bcstB[:, :],
                                               bcstD[:, :], ALU.mult),
            wait=[d["pbcB"], d["pbcD"]] + P1_DONE +
            ([stw_pairs[ch - 2]] if ch >= 2 else []) + REUSE_W,
            inc=vwd)
        add("scalar",
            lambda e, par=par: e.activation(cch[:, par, :],
                                            bcstC[:, :], AF.Copy,
                                            bias=0.0, scale=1.0),
            wait=[d["pbcC"]] + ([stc_pairs[ch - 2]] if ch >= 2 else []),
            inc=vwd)
        d["vwd_t"] = vwd.v
        stw_pairs.append(dma(
            "gpsimd", bcw_dram[par], wch[:, par, :], "stg",
            wait=[(vwd, d["vwd_t"])] +
            ([bigB_pairs[ch - 2][3]] if ch >= 2 else []), rot=2))
        stc_pairs.append(dma(
            "gpsimd", bcc_dram[par], cch[:, par, :], "stg",
            wait=[(vwd, d["vwd_t"])] +
            ([bigC_pairs[ch - 2][3]] if ch >= 2 else []), rot=2))
        bigB_pairs.append([dma(
            "sync",
            bigB[:, par, q * 4:(q + 1) * 4, :].rearrange("p a b -> p (a b)"),
            bcw_dram[par:par + 1, q * 4:(q + 1) * 4, :].rearrange(
                "x a b -> x (a b)").broadcast_to((128, 4 * TCH)),
            "bcb", wait=[stw_pairs[-1],
                         (vu, max(0, (ch - 2) * DS + (q + 1) * 4))],
            rot=4) for q in range(4)])
        bigC_pairs.append([dma(
            "sync",
            bigC[:, par, q * 4:(q + 1) * 4, :].rearrange("p a b -> p (a b)"),
            bcc_dram[par:par + 1, q * 4:(q + 1) * 4, :].rearrange(
                "x a b -> x (a b)").broadcast_to((128, 4 * TCH)),
            "bcc", wait=[stc_pairs[-1],
                         (vv, max(0, (ch - 2) * DS + (q + 1) * 4))],
            rot=4) for q in range(4)])

    def make_tail(ch, tz_t, tsr_t, pbank):
        par = ch % 2
        bat = ch // CHB
        chb = ch % CHB
        st = {}
        jobs = {}

        def silu():
            for j in range(2):
                add("scalar",
                    lambda e, j=j: e.activation(gsil[:, j, :], pC[:, j, :],
                                                AF.Silu, bias=0.0, scale=1.0),
                    wait=[(tz, tz_t), (va, max(0, ch))], inc=ssz)
            st["ssz_t"] = ssz.v

        def abuf_f():
            add("vector",
                lambda e: e.tensor_tensor(abuf[:, :, :], pbank[:, :, :],
                                          gsil[:, :, :], ALU.mult),
                wait=[(ssz, st["ssz_t"]), (tsr, tsr_t)], inc=va)
            st["va_t"] = va.v

        def wout(ntl):
            def f():
                add("tensor",
                    lambda e, ntl=ntl: e.matmul(
                        pC[:, ntl % 2, :],
                        woutTb[:, 0, ntl * 128:(ntl + 1) * 128], abuf[:, 0, :],
                        start=True, stop=False),
                    wait=[(va, st["va_t"]), (swo, max(0, ch * 8 + ntl - 1))],
                    inc=two)
                add("tensor",
                    lambda e, ntl=ntl: e.matmul(
                        pC[:, ntl % 2, :],
                        woutTb[:, 1, ntl * 128:(ntl + 1) * 128], abuf[:, 1, :],
                        start=False, stop=True),
                    inc=two)
                st["two_%d" % ntl] = two.v
            return f

        def evac(ntl):
            def f():
                gi = ch * 2 + ntl // 4
                add("scalar",
                    lambda e, ntl=ntl: e.activation(
                        attnst[:, ntl % 4, :], pC[:, ntl % 2, :], AF.Copy,
                        bias=0.0, scale=1.0),
                    wait=[(two, st["two_%d" % ntl])] +
                    ([grs_pairs[gi - 1]] if gi >= 1 else []), inc=swo)
                if ntl % 4 == 3:
                    grs_pairs.append(dma(
                        "gpsimd",
                        rs_in[bat][ch // 4,
                                   (ntl // 4) * 4:(ntl // 4) * 4 + 4, :,
                                   (ch % 4) * TCH:(ch % 4) * TCH +
                                   TCH].rearrange("a p t -> p a t"),
                        attnst[:, :, :], "grs",
                        wait=[(swo, swo.v)] + ZRS_DONE, rot=2))
                    if ntl == 7 and chb == CHB - 1:
                        add("gpsimd",
                            lambda e, bat=bat: e.collective_compute(
                                "ReduceScatter", ALU.add,
                                replica_groups=[list(range(NCORE))],
                                ins=[rs_in[bat]], outs=[rs_out[bat]]),
                            wait=grs_pairs[bat * CHB * 2:
                                           (bat + 1) * CHB * 2], inc=cc)
            return f

        jobs[0] = [silu]
        jobs[1] = [abuf_f]
        for ntl in range(8):
            jobs.setdefault(2 + ntl, []).append(wout(ntl))
            jobs.setdefault(3 + ntl, []).append(evac(ntl))
        return jobs

    def emit_u(chu, su):
        siu = chu * DS + su
        add("vector",
            lambda e, chu=chu, su=su, siu=siu: e.tensor_tensor(
                ubuf[:, siu % 2, :, :],
                x_inner[:, :, chu * TCH:(chu + 1) * TCH],
                bigB[:, chu % 2, su:su + 1, :].broadcast_to((128, 2, TCH)),
                ALU.mult),
            wait=[bigB_pairs[chu][su // 4], (ssi, P1T["ssi"])], inc=vu)

    pending = {}
    for ch in range(NCH):
        par = ch % 2
        bat = ch // CHB
        chb = ch % CHB
        t0 = ch * TCH
        sidx0 = ch * DS
        pbank = pB if ch % 2 == 0 else pA
        if ch == 0:
            emit_prologue_a(0)
            emit_prologue_b(0)
            emit_u(0, 0)
        pldz = dma("sync", rhsxn[:, par, :, :],
                   ag_xn_out[ch // 4].rearrange("(a p) t -> p a t", p=128)[
                       :, :, (ch % 4) * TCH:(ch % 4) * TCH + TCH],
                   "ldz", wait=[(tfr, P1T["tfr"]),
                                (tz, max(0, (ch - 1) * 16))], rot=2)
        add("tensor",
            lambda e: e.matmul(pD[:, :], o1f[0:1, :],
                                        bcstD[0:1, :],
                                        start=True, stop=True),
            wait=[(vwd, prol[ch]["vwd_t"]), (sda, max(0, ch * 32)),
                  (v_sx3, v_sx_done)], inc=td1)
        td1_t = td1.v
        if ch + 1 < NCH:
            emit_prologue_a(ch + 1)
        for s in range(DS):
            slot = s % 2
            si = sidx0 + s
            for j in range(2):
                add("scalar",
                    lambda e, j=j, s=s, slot=slot: e.activation(
                        dAb[:, slot, j, :], pD[:, :], AF.Exp, bias=0.0,
                        scale=a32[:, j, s:s + 1]),
                    wait=[(td1, td1_t), (vA, vA_done),
                          (vsc, max(0, (si - 1) * 2))],
                    inc=sda)
            sda_t = sda.v
            if s < DS - 1:
                emit_u(ch, s + 1)
            elif ch + 1 < NCH:
                emit_u(ch + 1, 0)
            vu_t = si + 1
            for j in range(2):
                if chb == 0:
                    add("vector",
                        lambda e, j=j, slot=slot, si=si:
                        e.tensor_tensor_scan(
                            hbuf[:, slot, j, :], dAb[:, slot, j, :],
                            ubuf[:, si % 2, j, :],
                            0.0, ALU.mult, ALU.add),
                        wait=[(sda, sda_t), (vu, vu_t),
                              (vv, max(0, si - 1))], inc=vsc)
                else:
                    add("vector",
                        lambda e, j=j, slot=slot, ch=ch, s=s, si=si:
                        e.tensor_tensor_scan(
                            hbuf[:, slot, j, :], dAb[:, slot, j, :],
                            ubuf[:, si % 2, j, :],
                            state[:, ch % 2, j, s:s + 1], ALU.mult, ALU.add),
                        wait=[(sda, sda_t), (vu, vu_t),
                              (vv, max(0, si - 1))], inc=vsc)
            vsc_t = vsc.v
            add("vector",
                lambda e, ch=ch, s=s, slot=slot: e.tensor_copy(
                    state[:, (ch + 1) % 2, :, s:s + 1],
                    hbuf[:, slot, :, TCH - 1:TCH]),
                inc=vst)
            if s >= 2:
                tsr_need = ch * 34 + 2 * (s - 2) + 2
            else:
                tsr_need = max(0, (ch - 1) * 34 + 2 * (14 + s) + 2)
            add("gpsimd",
                lambda e, slot=slot, par=par, s=s: e.tensor_tensor(
                    vbuf[:, slot, :, :], hbuf[:, slot, :, :],
                    bigC[:, par, s:s + 1, :].broadcast_to((128, 2, TCH)),
                    ALU.mult),
                wait=[(vsc, vsc_t), bigC_pairs[ch][s // 4],
                      (tsr, tsr_need)], inc=vv)
            vv_t = vv.v
            for j in range(2):
                add("tensor",
                    lambda e, s=s, slot=slot, j=j, pbank=pbank: e.matmul(
                        pbank[:, j, :], identb[:, :], vbuf[:, slot, j, :],
                        start=(s == 0), stop=False),
                    wait=[(vv, vv_t)] +
                    ([(va, max(0, ch - 1))] if s == 0 and j == 0 else []),
                    inc=tsr)
            if s == 1 and ch + 1 < NCH:
                emit_prologue_b(ch + 1)
            for f in pending.get(s, []):
                f()
        for j in range(2):
            add("tensor",
                lambda e, j=j, t0=t0, pbank=pbank: e.matmul(
                    pbank[:, j, :], ddiag[:, j, :], x_inner[:, j, t0:t0 + TCH],
                    start=False, stop=(j == 1)),
                wait=[(gdiag, gdiag_done)], inc=tsr)
        tsr_t = tsr.v

        for j in range(2):
            for k in range(8):
                add("tensor",
                    lambda e, j=j, k=k, par=par: e.matmul(
                        pC[:, j, :], winzTb[:, k, j * 128:(j + 1) * 128],
                        rhsxn[:, par, k, :], start=(k == 0), stop=(k == 7)),
                    wait=[pldz, (ssz, max(0, ch * 2)),
                          (swo, max(0, ch * 8)), (tdq, tdq_all)],
                    inc=tz)
        tz_t = tz.v
        pending = make_tail(ch, tz_t, tsr_t, pbank)
    for sl in sorted(pending):
        for f in pending[sl]:
            f()
    cc_rs = {0: cc_ssm[1] + 1, 1: cc_ssm[1] + 2}
    assert cc.v == cc_rs[1]
    P23T = {"tsr": tsr.v, "vu": vu.v, "vv": vv.v, "vsc": vsc.v, "va": va.v,
            "swo": swo.v, "two": two.v, "tz": tz.v, "ssz": ssz.v,
            "sda": sda.v, "td1": td1.v,
            "vst": vst.v}

    if DEBUG:
        dbg2 = SS("dbg")
        for j in range(2):
            dma("gpsimd", d_xi[j, :, :], x_inner[:, j, :], "dbg",
                wait=[(ssi, P1T["ssi"])], rot=2)
        for b in range(2):
            dma("gpsimd", d_ssm[b, :, :], ar_ssm_out[b][:, :], "dbg",
                wait=[(cc, cc_ssm[1])], rot=2)
        for c in range(8):
            for nt in range(8):
                dma("gpsimd", d_rsin[c, nt, :, :],
                    rs_in[1][c, nt, :, :], "dbg",
                    wait=P.dall("grs") + [(cc, cc_rs[1])], rot=2)
        for b in range(2):
            for nt in range(8):
                dma("gpsimd", d_rsout[b, nt, :, :], rs_out[b][nt, :, :],
                    "dbg", wait=[(cc, cc_rs[1])], rot=2)
        for r in range(8):
            for kt in range(8):
                dma("gpsimd", d_agu[r, kt, :, :], agu_out[r, kt, :, :],
                    "dbg", wait=[(cc, cc_agw)], rot=2)

    bigC_h.free()
    bigB_h.free()
    attnst_h.free()
    abuf_h.free()
    gsil_h.free()
    state_h.free()
    vbuf_h.free()
    hbuf_h.free()
    ubuf_h.free()
    dAb_h.free()
    cch_h.free()
    wch_h.free()
    bcstD_h.free()
    bcstC_h.free()
    bcstB_h.free()
    dspr_h.free()
    rhsxn_h.free()
    x_inner_h.free()

    sptile_h = _FSB(nc, "sptile", [128, 2, TCH], F32)
    macc_h = _FSB(nc, "macc", [128, 8, 8], F32)
    redt_h = _FSB(nc, "redt", [128, 4, 8], F32)
    lhsA_h = _FSB(nc, "lhsA", [128, 2, 8, 128], BF16)
    lhsA = lhsA_h.ap()
    sptile = sptile_h.ap()
    macc = macc_h.ap()
    redt = redt_h.ap()

    P23_DONE = (P.dall("dbg") if DEBUG else []) + [
        (tsr, P23T["tsr"]), (swo, P23T["swo"]), (va, P23T["va"]),
                (vu, P23T["vu"]), (vv, P23T["vv"]), (vsc, P23T["vsc"]),
                (sda, P23T["sda"]), (ssz, P23T["ssz"]),
                (vst, P23T["vst"])] + P.dall("grs") + P.dall("gsm")

    tff = SS("tff")
    sfv = SS("sfv")
    vfm = SS("vfm")
    gsp_pairs = []
    kqm_e_h = _FSB(nc, "kqm_e", [128, 8, TOK], BF16, side="left")
    kqm_e = kqm_e_h.ap()
    kqm = kqm_e
    PKQ_PAIRS = [dma("sync", kqm_e[:, k, :], kq_dram[k, :, :], "ldkq",
                     wait=P.dall("gkq") + P23_DONE, rot=4)
                 for k in range(8)]

    for wi, (agout, spill, acc_col) in enumerate(
            [(agg_out, gt_spill, 0), (agu_out, up_spill, 2)]):
        for nt in range(24):
            r, c0 = nt // 3, (nt % 3) * 128
            buf = nt % 2
            idx = wi * 24 + nt
            ldl_prs = []
            for kt in range(8):
                ldl_w = [(cc, cc_agw), (tff, max(0, (idx - 1) * 32))]
                if idx == 0 and kt == 0:
                    ldl_w = ldl_w + P23_DONE
                ldl_prs.append(dma(
                    "sync", lhsA[:, buf, kt, :], agout[r, kt, :, c0:c0 + 128],
                    "ldl", wait=ldl_w, rot=4))
            for m in range(4):
                pm = pA if m % 2 == 0 else pB
                mi = idx * 4 + m
                w = ldl_prs + PKQ_PAIRS + [(sfv, max(0, mi - 1))]
                if idx == 0 and m == 0:
                    w = w + P23_DONE
                for kt in range(8):
                    add("tensor",
                        lambda e, buf=buf, kt=kt, m=m, pm=pm: e.matmul(
                            pm[:, 0, :], lhsA[:, buf, kt, :],
                            kqm[:, kt, m * TCH:(m + 1) * TCH],
                            start=(kt == 0), stop=(kt == 7)),
                        wait=w, inc=tff)
                tff_t = tff.v
                add("scalar",
                    lambda e, m=m, pm=pm: e.activation(
                        sptile[:, m % 2, :], pm[:, 0, :], AF.Copy, bias=0.0,
                        scale=scal[:, 0, 0:1]),
                    wait=[(tff, tff_t)] +
                    ([gsp_pairs[mi - 2]] if mi >= 2 else []),
                    inc=sfv)
                sfv_t = sfv.v
                add("vector",
                    lambda e, m=m: e.reduce_max(redt[:, 0, 0:1],
                                                sptile[:, m % 2, :], AXX,
                                                apply_absolute_value=True),
                    wait=[(sfv, sfv_t)], inc=vfm)
                if nt == 0 and m == 0:
                    add("vector",
                        lambda e, acc_col=acc_col: e.tensor_copy(
                            macc[:, acc_col, 0:1], redt[:, 0, 0:1]),
                        inc=vfm)
                else:
                    add("vector",
                        lambda e, acc_col=acc_col: e.tensor_max(
                            macc[:, acc_col, 0:1],
                            macc[:, acc_col, 0:1], redt[:, 0, 0:1]),
                        inc=vfm)
                gsp_pairs.append(dma(
                    "gpsimd", spill[nt, :, m * TCH:(m + 1) * TCH],
                    sptile[:, m % 2, :], "gsp", wait=[(vfm, vfm.v)], rot=4))
    vfm_a = vfm.v
    GSP_DONE = P.dall("gsp")

    if DEBUG:
        dbg3 = SS("dbg")
        for nt in range(24):
            dma("gpsimd", d_up[nt, :, :], up_spill[nt, :, :], "dbg",
                wait=GSP_DONE, rot=2)
    t4 = SS("t4")
    v4 = SS("v4")
    add("tensor", lambda e: e.transpose(pD[0:1, 0:128], macc[:, 0, 0:1],
                                        identf[:, :]),
        wait=[(vfm, vfm_a), (sda, P23T["sda"])], inc=t4)
    add("vector", lambda e: e.memset(tiny[0:1, 0, 0:8], 0.0),
        wait=[(vfm, vfm_a)], inc=v4)
    add("vector", lambda e: e.reduce_max(tiny[0:1, 0, 0:1], pD[0:1, 0:128], AXX),
        wait=[(t4, t4.v)], inc=v4)
    add("tensor", lambda e: e.transpose(pD[0:1, 0:128], macc[:, 2, 0:1],
                                        identf[:, :]),
        wait=[(v4, v4.v)], inc=t4)
    add("vector", lambda e: e.reduce_max(tiny[0:1, 0, 1:2], pD[0:1, 0:128], AXX),
        wait=[(t4, t4.v)], inc=v4)
    pg4 = dma("gpsimd", ars_in[1][:, :], tiny[0:1, 0, 0:8], "gars",
              wait=[(v4, v4.v)])
    add("gpsimd",
        lambda e: e.collective_compute(
            "AllReduce", ALU.max, replica_groups=[list(range(NCORE))],
            ins=[ars_in[1]], outs=[ars_out[1]]),
        wait=[pg4], inc=cc)
    cc_s2 = cc.v
    pld4 = dma("sync", tiny[0:1, 4, 0:8], ars_out[1][:, :], "lars",
               wait=[(cc, cc_s2)])
    v5 = SS("v5")
    add("vector",
        lambda e: e.tensor_scalar(tiny[0:1, 0, 0:1], tiny[0:1, 4, 0:1], 1.0 / QMAX,
                                  1e-12, ALU.mult, ALU.max),
        wait=[pld4], inc=v5)
    add("vector",
        lambda e: e.tensor_scalar(tiny[0:1, 1, 0:1], tiny[0:1, 4, 1:2], 1.0 / QMAX,
                                  1e-12, ALU.mult, ALU.max), inc=v5)
    t5 = SS("t5")
    add("tensor", lambda e: e.matmul(pD[:, 0:2], o1f[0:1, :], tiny[0:1, 0:2, 0:1],
                                     start=True, stop=True),
        wait=[(v5, v5.v)], inc=t5)
    add("vector", lambda e: e.tensor_copy(scal[:, 2, 0:1], pD[:, 0:1]),
        wait=[(t5, t5.v)], inc=v5)
    add("vector", lambda e: e.tensor_copy(scal[:, 4, 0:1], pD[:, 1:2]), inc=v5)
    add("vector", lambda e: e.reciprocal(scal[:, 3, 0:1], scal[:, 2, 0:1]), inc=v5)
    add("vector", lambda e: e.reciprocal(scal[:, 5, 0:1], scal[:, 4, 0:1]), inc=v5)
    v5_done = v5.v

    lhsA_h.free()
    kqm_e_h.free()
    kgar_h = _FSB(nc, "kgar", [128, 24, TOK], I16)
    mt_h = _FSB(nc, "mt", [128, TCH], F32)
    kut_h = _FSB(nc, "kut", [128, 2, TCH], I16)
    kst_h = _FSB(nc, "kst", [128, TCH], U16)
    sigt_h = _FSB(nc, "sigt", [128, TCH], F32)
    gtile_h = _FSB(nc, "gtile", [128, 2, TCH], F32)
    kgar = kgar_h.ap()
    mt = mt_h.ap()
    kut = kut_h.ap()
    kst = kst_h.ap()
    sigt = sigt_h.ap()
    gtile = gtile_h.ap()
    vkg = SS("vkg")
    for nt in range(24):
        for m in range(4):
            buf = (nt * 4 + m) % 2
            pgt = dma("sync", gtile[:, buf, :],
                      gt_spill[nt, :, m * TCH:(m + 1) * TCH],
                      "ldsp", wait=GSP_DONE +
                      [(vkg, max(0, nt * 4 + m - 1))], rot=2)
            add("vector",
                lambda e, nt=nt, m=m, buf=buf: e.tensor_scalar(
                    kgar[:, nt, m * TCH:(m + 1) * TCH], gtile[:, buf, :],
                    scal[:, 3, 0:1], None, ALU.mult),
                wait=[pgt, (v5, v5_done)], inc=vkg)
    vkg_done = vkg.v

    gtile_h.free()
    ssg = SS("ssg")
    vme = SS("vme")
    for nt in range(24):
        for m in range(4):
            i = nt * 4 + m
            add("scalar",
                lambda e, nt=nt, m=m: e.activation(
                    sigt[:, :], kgar[:, nt, m * TCH:(m + 1) * TCH], AF.Sigmoid,
                    bias=0.0, scale=scal[:, 2, 0:1]),
                wait=[(vkg, vkg_done), (vme, max(0, (i - 1) * 4 + 2))],
                inc=ssg)
            ssg_t = ssg.v
            add("vector",
                lambda e: e.tensor_scalar(kst[:, :], sigt[:, :], 65536.0,
                                          None, ALU.mult),
                wait=[(ssg, ssg_t)], inc=vme)
            add("vector",
                lambda e, nt=nt, m=m: e.tensor_tensor(
                    mt[:, :], kgar[:, nt, m * TCH:(m + 1) * TCH], kst[:, :],
                    ALU.mult), inc=vme)
            add("vector",
                lambda e: e.reduce_max(redt[:, 1, 0:1], mt[:, :], AXX,
                                       apply_absolute_value=True), inc=vme)
            if i == 0:
                add("vector", lambda e: e.tensor_copy(macc[:, 4, 0:1],
                                                      redt[:, 1, 0:1]), inc=vme)
            else:
                add("vector", lambda e: e.tensor_max(macc[:, 4, 0:1],
                                                     macc[:, 4, 0:1],
                                                     redt[:, 1, 0:1]), inc=vme)
    vme_done = vme.v
    t6 = SS("t6")
    add("tensor", lambda e: e.transpose(pD[0:1, 0:128], macc[:, 4, 0:1],
                                        identf[:, :]),
        wait=[(vme, vme_done)], inc=t6)
    v6 = SS("v6")
    add("vector", lambda e: e.memset(tiny[0:1, 0, 0:8], 0.0),
        wait=[(vme, vme_done)], inc=v6)
    add("vector", lambda e: e.reduce_max(tiny[0:1, 0, 0:1], pD[0:1, 0:128], AXX),
        wait=[(t6, t6.v)], inc=v6)
    pg5 = dma("gpsimd", ars_in[2][:, :], tiny[0:1, 0, 0:8], "gars",
              wait=[(v6, v6.v)])
    add("gpsimd",
        lambda e: e.collective_compute(
            "AllReduce", ALU.max, replica_groups=[list(range(NCORE))],
            ins=[ars_in[2]], outs=[ars_out[2]]),
        wait=[pg5], inc=cc)
    cc_s3 = cc.v
    pld5 = dma("sync", tiny[0:1, 4, 0:8], ars_out[2][:, :], "lars",
               wait=[(cc, cc_s3)])
    add("vector",
        lambda e: e.scalar_tensor_tensor(tiny[0:1, 0, 0:1], tiny[0:1, 4, 0:1],
                                         1.0 / (65536.0 * QMAX),
                                         scal[0:1, 2, 0:1], ALU.mult, ALU.mult),
        wait=[pld5], inc=v6)
    add("vector",
        lambda e: e.tensor_scalar(tiny[0:1, 1, 0:1], tiny[0:1, 0, 0:1], 1.0, 1e-12,
                                  ALU.mult, ALU.max), inc=v6)
    add("vector", lambda e: e.reciprocal(tiny[0:1, 2, 0:1], tiny[0:1, 1, 0:1]),
        inc=v6)
    add("vector",
        lambda e: e.scalar_tensor_tensor(tiny[0:1, 2, 0:1], tiny[0:1, 2, 0:1],
                                         1.0 / 65536.0, scal[0:1, 2, 0:1],
                                         ALU.mult, ALU.mult), inc=v6)
    add("tensor", lambda e: e.matmul(pD[:, 0:3], o1f[0:1, :], tiny[0:1, 0:3, 0:1],
                                     start=True, stop=True),
        wait=[(v6, v6.v)], inc=t6)
    add("vector", lambda e: e.tensor_copy(scal[:, 6, 0:1], pD[:, 2:3]),
        wait=[(t6, t6.v)], inc=v6)
    add("vector", lambda e: e.tensor_copy(scal[:, 7, 0:1], pD[:, 1:2]), inc=v6)
    v6_done = v6.v

    vg1 = SS("vg1")
    sg1 = SS("sg1")
    VG1 = 7
    for nt in range(24):
        for m in range(4):
            i = nt * 4 + m
            buf = i % 2
            add("scalar",
                lambda e, nt=nt, m=m: e.activation(
                    sigt[:, :], kgar[:, nt, m * TCH:(m + 1) * TCH], AF.Sigmoid,
                    bias=0.0, scale=scal[:, 2, 0:1]),
                wait=[(v6, v6_done), (vg1, max(0, (i - 1) * VG1 + 4))],
                inc=sg1)
            sg1_t = sg1.v
            add("vector",
                lambda e: e.tensor_scalar(kst[:, :], sigt[:, :], 65536.0,
                                          None, ALU.mult),
                wait=[(sg1, sg1_t)], inc=vg1)
            add("vector",
                lambda e, nt=nt, m=m: e.tensor_tensor(
                    mt[:, :], kgar[:, nt, m * TCH:(m + 1) * TCH], kst[:, :],
                    ALU.mult), inc=vg1)
            add("vector",
                lambda e, nt=nt, m=m: e.tensor_scalar(
                    kgar[:, nt, m * TCH:(m + 1) * TCH], mt[:, :],
                    scal[:, 6, 0:1], None, ALU.mult), inc=vg1)
            pldu = dma("sync", sptile[:, buf, :],
                       up_spill[nt, :, m * TCH:(m + 1) * TCH],
                       "ldsp", wait=[(vg1, max(0, (i - 1) * VG1 + 5))] +
                       GSP_DONE, rot=2)
            add("vector",
                lambda e, buf=buf: e.tensor_scalar(
                    kut[:, 0, :], sptile[:, buf, :], scal[:, 5, 0:1], None,
                    ALU.mult),
                wait=[pldu], inc=vg1)
            add("vector",
                lambda e, nt=nt, m=m: e.tensor_tensor(
                    mt[:, :], kgar[:, nt, m * TCH:(m + 1) * TCH],
                    kut[:, 0, :], ALU.mult), inc=vg1)
            add("vector",
                lambda e: e.reduce_max(redt[:, 2, 0:1], mt[:, :], AXX,
                                       apply_absolute_value=True), inc=vg1)
            if i == 0:
                add("vector", lambda e: e.tensor_copy(macc[:, 5, 0:1],
                                                      redt[:, 2, 0:1]), inc=vg1)
            else:
                add("vector", lambda e: e.tensor_max(macc[:, 5, 0:1],
                                                     macc[:, 5, 0:1],
                                                     redt[:, 2, 0:1]), inc=vg1)
            assert vg1.v == i * VG1 + VG1
    vg1_done = vg1.v
    t7 = SS("t7")
    add("tensor", lambda e: e.transpose(pD[0:1, 0:128], macc[:, 5, 0:1],
                                        identf[:, :]),
        wait=[(vg1, vg1_done)], inc=t7)
    v8 = SS("v8")
    add("vector", lambda e: e.memset(tiny[0:1, 0, 0:8], 0.0),
        wait=[(vg1, vg1_done)], inc=v8)
    add("vector", lambda e: e.reduce_max(tiny[0:1, 0, 0:1], pD[0:1, 0:128], AXX),
        wait=[(t7, t7.v)], inc=v8)
    pg6 = dma("gpsimd", ars_in[3][:, :], tiny[0:1, 0, 0:8], "gars",
              wait=[(v8, v8.v)])
    add("gpsimd",
        lambda e: e.collective_compute(
            "AllReduce", ALU.max, replica_groups=[list(range(NCORE))],
            ins=[ars_in[3]], outs=[ars_out[3]]),
        wait=[pg6], inc=cc)
    cc_s4 = cc.v
    pld6 = dma("sync", tiny[0:1, 4, 0:8], ars_out[3][:, :], "lars",
               wait=[(cc, cc_s4)])
    add("vector",
        lambda e: e.scalar_tensor_tensor(tiny[0:1, 0, 0:1], tiny[0:1, 4, 0:1],
                                         1.0 / QMAX, scal[0:1, 7, 0:1],
                                         ALU.mult, ALU.mult),
        wait=[pld6], inc=v8)
    add("vector",
        lambda e: e.scalar_tensor_tensor(tiny[0:1, 0, 0:1], tiny[0:1, 0, 0:1], 1.0,
                                         scal[0:1, 4, 0:1], ALU.mult, ALU.mult),
        inc=v8)
    add("vector",
        lambda e: e.tensor_scalar(tiny[0:1, 0, 0:1], tiny[0:1, 0, 0:1], 1.0, 1e-12,
                                  ALU.mult, ALU.max), inc=v8)
    add("vector", lambda e: e.reciprocal(tiny[0:1, 1, 0:1], tiny[0:1, 0, 0:1]),
        inc=v8)
    add("vector",
        lambda e: e.scalar_tensor_tensor(tiny[0:1, 1, 0:1], tiny[0:1, 1, 0:1], 1.0,
                                         scal[0:1, 7, 0:1], ALU.mult, ALU.mult),
        inc=v8)
    add("vector",
        lambda e: e.scalar_tensor_tensor(tiny[0:1, 1, 0:1], tiny[0:1, 1, 0:1], 1.0,
                                         scal[0:1, 4, 0:1], ALU.mult, ALU.mult),
        inc=v8)
    add("tensor", lambda e: e.matmul(pD[:, 0:2], o1f[0:1, :], tiny[0:1, 0:2, 0:1],
                                     start=True, stop=True),
        wait=[(v8, v8.v)], inc=t7)
    add("vector", lambda e: e.tensor_copy(scal[:, 8, 0:1], pD[:, 0:1]),
        wait=[(t7, t7.v)], inc=v8)
    add("vector", lambda e: e.tensor_copy(scal[:, 9, 0:1], pD[:, 1:2]), inc=v8)
    v8_done = v8.v

    if DEBUG:
        dma("gpsimd", d_scal[:, :], scal[:, :], "dbg",
            wait=[(v8, v8_done)])

    sigt_h.free()
    kst_h.free()
    qoar_h = _FSB(nc, "qoar", [128, 24, TCH], BF16)
    qoi_h = _FSB(nc, "qoi", [128, TCH], I16)
    ffm_h = _FSB(nc, "ffm", [128, 8, TCH], F32)
    lhsD_h = _FSB(nc, "lhsD", [128, 2, 24, 128], BF16)
    attt_h = _FSB(nc, "attt", [128, 2, 8, 128], BF16)
    sumt_h = _FSB(nc, "sumt", [128, 8, 128], F32)
    outt_h = _FSB(nc, "outt", [128, 2, DM], F32)
    xrt_h = _FSB(nc, "xrt", [128, 2, DM], F32)
    qoar = qoar_h.ap()
    qoi = qoi_h.ap()
    ffm = ffm_h.ap()
    lhsD = lhsD_h.ap()
    attt = attt_h.ap()
    sumt = sumt_h.ap()
    outt = outt_h.ap()
    xrt = xrt_h.ap()

    vg2 = SS("vg2")
    tdn = SS("tdn")
    sdn = SS("sdn")
    vfin = SS("vfin")
    tfin = SS("tfin")
    gout_pairs = []
    VG2 = 4

    for m in range(4):
        for kt in range(24):
            i = m * 24 + kt
            buf = i % 2
            pldu2 = dma("sync", sptile[:, buf, :],
                        up_spill[kt, :, m * TCH:(m + 1) * TCH],
                        "ldsp",
                        wait=[(vg2, max(0, (i - 1) * VG2 + 2)),
                              (vg1, vg1_done)], rot=2)
            add("vector",
                lambda e, buf=buf: e.tensor_scalar(
                    kut[:, 1, :], sptile[:, buf, :], scal[:, 5, 0:1], None,
                    ALU.mult),
                wait=[pldu2, (v8, v8_done)], inc=vg2)
            add("vector",
                lambda e, kt=kt, m=m: e.tensor_tensor(
                    mt[:, :], kgar[:, kt, m * TCH:(m + 1) * TCH], kut[:, 1, :],
                    ALU.mult),
                wait=[(tdn, max(0, m * 8 * 24))], inc=vg2)
            add("vector",
                lambda e: e.tensor_scalar(qoi[:, :], mt[:, :], scal[:, 9, 0:1],
                                          None, ALU.mult), inc=vg2)
            add("vector",
                lambda e, kt=kt: e.tensor_copy(qoar[:, kt, :], qoi[:, :]),
                inc=vg2)
            assert vg2.v == i * VG2 + VG2
        vg2_m = vg2.v
        for ntl in range(8):
            bufl = ntl % 2
            ldl2_prs = []
            for kt in range(24):
                ldl2_prs.append(dma(
                    "sync", lhsD[:, bufl, kt, :], agd_out[ntl, kt, :, :],
                    "ldl2",
                    wait=[(cc, cc_agw),
                          (tdn, max(0, (m * 8 + ntl - 1) * 24))], rot=4))
            for kt in range(24):
                add("tensor",
                    lambda e, bufl=bufl, kt=kt, ntl=ntl, m=m: e.matmul(
                        pE[:, :], lhsD[:, bufl, kt, :], qoar[:, kt, :],
                        start=(kt == 0), stop=(kt == 23)),
                    wait=ldl2_prs + [(vg2, vg2_m),
                          (sdn, max(0, m * 8 + ntl)),
                          (tsm, P1T["tsm"])],
                    inc=tdn)
            tdn_t = tdn.v
            add("scalar",
                lambda e, ntl=ntl: e.activation(ffm[:, ntl, :], pE[:, :],
                                                AF.Copy, bias=0.0,
                                                scale=scal[:, 8, 0:1]),
                wait=[(tdn, tdn_t), (vfin, max(0, m * 4 * 3))], inc=sdn)
        sdn_m = sdn.v
        for t4i in range(4):
            tt = m * 4 + t4i
            fbuf = tt % 2
            pats = []
            for b in range(2):
                pats.append(dma(
                    "sync", attt[:, b, :, :],
                    rs_out[b][:, :, tt * 128:(tt + 1) * 128].rearrange(
                        "a p t -> p a t"),
                    "ldfin", wait=[(cc, cc_rs[1]),
                                  (vfin, max(0, (tt - 1) * 3 + 1))], rot=2))
            pxr = dma("sync", xrt[:, fbuf, :],
                      x_tok[tt * 128:(tt + 1) * 128, :],
                      "ldfin", wait=[(vfin, max(0, (tt - 2) * 3 + 3))], rot=2)
            add("vector",
                lambda e: e.tensor_add(sumt[:, :, :], attt[:, 0, :, :],
                                       attt[:, 1, :, :]),
                wait=pats + [(sdn, sdn_m),
                      (tfin, max(0, (tt - 1) * 8))], inc=vfin)
            add("vector",
                lambda e, t4i=t4i: e.tensor_add(
                    sumt[:, :, :], sumt[:, :, :],
                    ffm[:, :, t4i * 128:(t4i + 1) * 128]), inc=vfin)
            vfin_t = vfin.v
            for a8 in range(8):
                add("tensor",
                    lambda e, a8=a8: e.transpose(
                        pC[:, a8 // 4, (a8 % 4) * 128:(a8 % 4 + 1) * 128],
                        sumt[:, a8, :], identf[:, :]),
                    wait=[(vfin, vfin_t), (swo, P23T["swo"]),
                          (vfin, max(0, (tt - 1) * 3 + 3))],
                    inc=tfin)
            tfin_t = tfin.v
            add("vector",
                lambda e, fbuf=fbuf: e.tensor_add(
                    outt[:, fbuf, :],
                    pC.rearrange("p a t -> p (a t)"), xrt[:, fbuf, :]),
                wait=[(tfin, tfin_t), pxr] +
                ([gout_pairs[tt - 2]] if tt >= 2 else []), inc=vfin)
            vfin_t2 = vfin.v
            gout_pairs.append(dma(
                "gpsimd", y_out[tt * 128:(tt + 1) * 128, :], outt[:, fbuf, :],
                "gout", wait=[(vfin, vfin_t2)], rot=2))

    add("gpsimd", lambda e: e.engine_nop(), wait=P.dall("gout"))

    with nc.Block() as block:
        @block.sync
        def _(eng):
            P.emit_engine("sync", eng)

        @block.gpsimd
        def _(eng):
            P.emit_engine("gpsimd", eng)

        @block.vector
        def _(eng):
            P.emit_engine("vector", eng)

        @block.scalar
        def _(eng):
            P.emit_engine("scalar", eng)

        @block.tensor
        def _(eng):
            P.emit_engine("tensor", eng)

    es.close()
    return nc


_CACHED = {}


def _get_nc():
    if "nc" not in _CACHED:
        info = {}
        build_program(collect=info)
        edges = compute_edges(info)
        _CACHED["nc"] = build_program(edges=edges)
    return _CACHED["nc"]


def kernel(L0_attn_x, L0_attn_W_in, L0_attn_conv_w, L0_attn_conv_b,
           L0_attn_W_x, L0_attn_A_log, L0_attn_D, L0_attn_W_out,
           L0_ffn_w_gate, L0_ffn_w_up, L0_ffn_w_down):
    x = np.ascontiguousarray(np.asarray(L0_attn_x, dtype=np.float32))
    W_in = np.ascontiguousarray(np.asarray(L0_attn_W_in, dtype=np.float32))
    conv_w = np.ascontiguousarray(np.asarray(L0_attn_conv_w, dtype=np.float32))
    conv_b = np.ascontiguousarray(np.asarray(L0_attn_conv_b, dtype=np.float32))
    W_x = np.ascontiguousarray(np.asarray(L0_attn_W_x, dtype=np.float32))
    A_log = np.ascontiguousarray(np.asarray(L0_attn_A_log, dtype=np.float32))
    D = np.ascontiguousarray(np.asarray(L0_attn_D, dtype=np.float32))
    W_out = np.ascontiguousarray(np.asarray(L0_attn_W_out, dtype=np.float32))
    w_gate = np.ascontiguousarray(np.asarray(L0_ffn_w_gate, dtype=np.float32))
    w_up = np.ascontiguousarray(np.asarray(L0_ffn_w_up, dtype=np.float32))
    w_down = np.ascontiguousarray(np.asarray(L0_ffn_w_down, dtype=np.float32))

    xf = x.reshape(T, DM)

    ident = np.eye(128, dtype=ml_dtypes.bfloat16)
    identf = np.eye(128, dtype=np.float32)
    onesf = np.ones((1, 128), dtype=np.float32)
    ones16 = np.ones((1, 16), dtype=np.float32)

    in_maps = []
    for c in range(NCORE):
        dsl = slice(c * DSL, (c + 1) * DSL)
        osl = slice(c * OSL, (c + 1) * OSL)
        im = {
            "x_tok": xf[c * TOK:(c + 1) * TOK, :],
            "winxT": np.ascontiguousarray(W_in[dsl, :].T),
            "winzT": np.ascontiguousarray(W_in[DI + c * DSL:DI + (c + 1) * DSL, :].T),
            "wxT": np.ascontiguousarray(W_x[:, dsl].T),
            "woutT": np.ascontiguousarray(W_out[:, dsl].T),
            "convw_in": conv_w[dsl, 0, :],
            "convb_in": conv_b[dsl].reshape(DSL, 1),
            "alog_in": A_log[dsl, :],
            "dvec_in": D[dsl].reshape(DSL, 1),
            "wup_rows": w_up[osl, :],
            "wgt_rows": w_gate[osl, :],
            "wdn_rows": w_down[c * 128:(c + 1) * 128, :],
            "ident_in": ident,
            "identf_in": identf,
            "onesf_in": onesf,
            "ones16_in": ones16,
        }
        in_maps.append(im)

    nc = _get_nc()
    res = run_bass_kernel_spmd(nc, in_maps, core_ids=list(range(NCORE)))
    out = np.concatenate([np.asarray(res.results[c]["y_out"])
                          for c in range(NCORE)], axis=0)
    return out.reshape(B, L, DM)


# version 14 (final)
# speedup vs baseline: 1.0230x; 1.0230x over previous
import numpy as np
import ml_dtypes
from contextlib import ExitStack

import concourse.bass as bass
import concourse.mybir as mybir
from concourse.bass_utils import run_bass_kernel_spmd

F32 = mybir.dt.float32
BF16 = mybir.dt.bfloat16
I16 = mybir.dt.int16
U16 = mybir.dt.uint16
AF = mybir.ActivationFunctionType
ALU = mybir.AluOpType
AXX = mybir.AxisListType.X

B, L, DM = 2, 8192, 1024
T = B * L
NCORE = 8
TOK = T // NCORE
DI = 2048
DSL = DI // NCORE
DS = 16
DCONV = 4
IFF = 3072
OSL = IFF // NCORE
QMAX = 32767.0
EPS = 1e-5
TCH = 512
NCH = T // TCH
CHB = NCH // 2
NT0 = TOK // 128


class Sem:
    __slots__ = ("h", "v", "name")

    def __init__(self, h, name):
        self.h = h
        self.v = 0
        self.name = name


class Prog:

    def __init__(self, nc, es, collect=None, edges=None):
        self.nc = nc
        self.es = es
        self.items = {"sync": [], "gpsimd": [], "vector": [], "scalar": [],
                      "tensor": []}
        self.sems = {}
        self.collect = collect
        self.edges = edges

    MERGE = {n: "misc" for n in
             ["sA", "vA", "gz", "gdiag", "t_sx", "v_sx", "v_sx2", "v_sx3",
              "t_sxb", "t4", "v4", "t5", "v5", "t6", "v6", "t7", "v8",
              "vdq", "sdqe", "tdq", "vqx", "twd", "tsm", "two", "vfm",
              "ssg", "sg1", "tfin"]}

    def sem(self, name):
        name = self.MERGE.get(name, name)
        if name not in self.sems:
            self.sems[name] = Sem(
                self.es.enter_context(self.nc.semaphore(name)), name)
        return self.sems[name]

    def add(self, eng, fn, wait=(), inc=None, incn=1):
        post = inc.v + incn if inc is not None else None
        self.items[eng].append((fn, list(wait), inc, incn, post))
        if inc is not None:
            inc.v = post
        return post

    def dma(self, eng, out, in_, stream, wait=(), rot=2):
        if not hasattr(self, "dcnt"):
            self.dcnt = {}
            self.dma_hist = {}
        i = self.dcnt.get(stream, 0)
        self.dcnt[stream] = i + 1
        j = i % rot
        sem = self.sem(f"{stream}_{j}")
        selfgate = [(sem, (i // rot) * 16)] if i >= rot else []
        self.add(eng, lambda e, o=out, i_=in_: e.dma_start(out=o, in_=i_),
                 wait=list(wait) + selfgate, inc=sem, incn=16)
        pair = (sem, (i // rot + 1) * 16)
        self.dma_hist.setdefault(stream, []).append(pair)
        return pair

    def dall(self, stream):
        out = {}
        for (sem, val) in self.dma_hist.get(stream, []):
            out[sem.name] = (sem, val)
        return list(out.values())

    @staticmethod
    def _rw(bins):
        if bins is None or not hasattr(bins, "ins"):
            return None
        inst = bins.ins
        tn = type(inst).__name__
        if ("DMA" in tn or "Dma" in tn or "Collective" in tn
                or "Sem" in tn or "Nop" in tn):
            return None

        def rng(op):
            if not hasattr(op, "memref") or op.memref is None:
                return None
            try:
                dtsz = mybir.dt.size(op.dtype)
                apl = list(op.ap)
                span = sum(abs(s) * (c - 1) for s, c in apl[1:])
                lo = op.offset * dtsz
                return (str(op.memref), lo, lo + (span + 1) * dtsz)
            except Exception:
                return None

        reads = [r for r in (rng(o) for o in inst.ins) if r is not None]
        writes = [r for r in (rng(o) for o in inst.outs) if r is not None]
        return (reads, writes)

    def emit_engine(self, ename, eng):
        watermark = {}
        edges = self.edges or {}
        pre = edges.get("pre", {})
        ainc = edges.get("inc", {})
        asem = None
        need_any_auto = any((ename, i) in ainc for i in range(len(self.items[ename])))
        if self.edges is not None and ename != "tensor" and need_any_auto:
            asem = self.sem("auto_" + ename)
        for idx, (fn, waits, inc, incn, post) in enumerate(self.items[ename]):
            allw = list(waits)
            for (sname, val) in pre.get((ename, idx), ()):
                allw.append((self.sem(sname), val))
            for (sem, val) in allw:
                if val > 0 and watermark.get(sem.name, -1) < val:
                    eng.wait_ge(sem.h, val)
                    watermark[sem.name] = val
            ins = fn(eng)
            need_auto = (ename, idx) in ainc
            if inc is not None:
                if ins is None:
                    eng.sem_inc(inc.h, incn)
                else:
                    ins.then_inc(inc.h, incn)
                assert not need_auto
            elif need_auto:
                if ins is None:
                    eng.sem_inc(asem.h, 1)
                else:
                    ins.then_inc(asem.h, 1)
            if self.collect is not None and ename != "tensor":
                rec = self._rw(ins)
                self.collect.setdefault(ename, {})[idx] = (
                    rec, inc.name if inc is not None else None, post)


def compute_edges(info):
    LINE = 32
    pre = {}
    incs = {}
    for ename, ops in info.items():
        writes_by_tensor = {}
        dep = {}
        for idx in sorted(ops):
            rec = ops[idx]
            rw = rec[0]
            if rw is None:
                continue
            reads, writes = rw
            need = -1
            for (t, lo, hi) in reads:
                for (j, lo2, hi2) in writes_by_tensor.get(t, ()):
                    if lo < hi2 and lo2 < hi:
                        need = max(need, j)
            for (t, lo, hi) in writes:
                llo, lhi = lo // LINE, (hi + LINE - 1) // LINE
                for (j, lo2, hi2) in writes_by_tensor.get(t, ()):
                    if llo < (hi2 + LINE - 1) // LINE and lo2 // LINE < lhi:
                        need = max(need, j)
            if need >= 0:
                dep[idx] = need
            for (t, lo, hi) in writes:
                writes_by_tensor.setdefault(t, []).append((idx, lo, hi))
        auto_prods = sorted({j for j in dep.values() if ops[j][1] is None})
        rank = {j: k + 1 for k, j in enumerate(auto_prods)}
        for j in auto_prods:
            incs[(ename, j)] = True
        for idx, j in dep.items():
            if ops[j][1] is not None:
                pre.setdefault((ename, idx), []).append((ops[j][1], ops[j][2]))
            else:
                pre.setdefault((ename, idx), []).append(
                    ("auto_" + ename, rank[j]))
    return {"pre": pre, "inc": incs}



class _FSB:

    def __init__(self, nc, name, shape, dt, side="right"):
        self.cm = nc.sbuf_tensor(name, shape, dt, side=side)
        self.h = self.cm.__enter__()

    def ap(self):
        return self.h.ap()

    def free(self):
        self.cm.__exit__(None, None, None)


DEBUG = False


def build_program(collect=None, edges=None):
    nc = bass.Bass("TRN2", target_bir_lowering=False,
                   detect_race_conditions=False)

    def din(name, shape, dt=F32):
        return nc.dram_tensor(name, shape, dt, kind="ExternalInput").ap()

    x_tok = din("x_tok", [TOK, DM])
    winxT = din("winxT", [DM, DSL])
    winzT = din("winzT", [DM, DSL])
    wxT = din("wxT", [DSL, 33])
    woutT = din("woutT", [DSL, DM])
    convw_in = din("convw_in", [DSL, DCONV])
    convb_in = din("convb_in", [DSL, 1])
    alog_in = din("alog_in", [DSL, DS])
    dvec_in = din("dvec_in", [DSL, 1])
    wup_rows = din("wup_rows", [OSL, DM])
    wgt_rows = din("wgt_rows", [OSL, DM])
    wdn_rows = din("wdn_rows", [128, IFF])
    ident_in = din("ident_in", [128, 128], BF16)
    identf_in = din("identf_in", [128, 128])
    onesf_in = din("onesf_in", [1, 128])
    ones16_in = din("ones16_in", [1, 16])

    y_out = nc.dram_tensor("y_out", [TOK, DM], F32, kind="ExternalOutput").ap()
    if DEBUG:
        d_xn = nc.dram_tensor("d_xn", [8, 128, TOK], F32,
                              kind="ExternalOutput").ap()
        d_lnst = nc.dram_tensor("d_lnst", [NT0, 128, 8, 8], F32,
                                kind="ExternalOutput").ap()
        d_sx = nc.dram_tensor("d_sx", [4, 128], F32,
                              kind="ExternalOutput").ap()
        d_dsp = nc.dram_tensor("d_dsp", [2, L], F32,
                               kind="ExternalOutput").ap()
        d_a32 = nc.dram_tensor("d_a32", [128, 2, DS], F32,
                               kind="ExternalOutput").ap()
        d_scal = nc.dram_tensor("d_scal", [128, 16, 8], F32,
                                kind="ExternalOutput").ap()
        d_kq = nc.dram_tensor("d_kq", [8, 128, TOK], BF16,
                              kind="ExternalOutput").ap()
        d_xi = nc.dram_tensor("d_xi", [2, 128, T], BF16,
                              kind="ExternalOutput").ap()
        d_ssm = nc.dram_tensor("d_ssm", [2, 33, L], F32,
                               kind="ExternalOutput").ap()
        d_rsin = nc.dram_tensor("d_rsin", [8, 8, 128, TOK], BF16,
                                kind="ExternalOutput").ap()
        d_rsout = nc.dram_tensor("d_rsout", [2, 8, 128, TOK], BF16,
                                 kind="ExternalOutput").ap()
        d_up = nc.dram_tensor("d_up", [24, 128, TOK], F32,
                              kind="ExternalOutput").ap()
        d_agu = nc.dram_tensor("d_agu", [8, 8, 128, OSL], BF16,
                               kind="ExternalOutput").ap()
        dbg = SS("dbg") if False else None

    def dint(name, shape, dt=F32):
        return nc.dram_tensor(name, shape, dt).ap()

    ag_xn_in = [dint("ag_xn_in%d" % q, [DM, 512], BF16) for q in range(4)]
    ag_xn_out = [dint("ag_xn_out%d" % q, [NCORE, DM, 512], BF16)
                 for q in range(4)]
    agu_in = dint("agu_in", [8, 128, OSL], BF16)
    agu_out = dint("agu_out", [NCORE, 8, 128, OSL], BF16)
    agg_in = dint("agg_in", [8, 128, OSL], BF16)
    agg_out = dint("agg_out", [NCORE, 8, 128, OSL], BF16)
    agd_in = dint("agd_in", [24, 128, 128], BF16)
    agd_out = dint("agd_out", [NCORE, 24, 128, 128], BF16)
    ar_ssm_in = [dint(f"ar_ssm_in{b}", [33, L]) for b in range(2)]
    ar_ssm_out = [dint(f"ar_ssm_out{b}", [33, L]) for b in range(2)]
    dsp_dram = [dint(f"dsp{b}", [1, L]) for b in range(2)]
    ars_in = [dint(f"ars_in{i}", [1, 8]) for i in range(4)]
    ars_out = [dint(f"ars_out{i}", [1, 8]) for i in range(4)]
    rs_in = [dint(f"rs_in{b}", [NCORE, 8, 128, TOK], BF16) for b in range(2)]
    rs_out = [dint(f"rs_out{b}", [8, 128, TOK], BF16) for b in range(2)]
    kq_dram = dint("kq_dram", [8, 128, TOK], BF16)
    up_spill = dint("up_spill", [24, 128, TOK])
    gt_spill = dint("gt_spill", [24, 128, TOK])

    es = ExitStack()
    P = Prog(nc, es, collect=collect, edges=edges)
    SS = P.sem
    add = P.add
    dma = P.dma

    def sbp(name, shape, dt):
        return es.enter_context(nc.sbuf_tensor(name, shape, dt)).ap()

    winxTb = sbp("winxTb", [128, 8, DSL], BF16)
    winzTb = sbp("winzTb", [128, 8, DSL], BF16)
    wxTb = sbp("wxTb", [128, 2, 33], BF16)
    woutTb = sbp("woutTb", [128, 2, DM], BF16)
    convdiag = sbp("convdiag", [128, 2, DCONV, 128], BF16)
    ddiag = sbp("ddiag", [128, 2, 128], BF16)
    identb = sbp("identb", [128, 128], BF16)
    identf = sbp("identf", [128, 128], F32)
    o1f = sbp("o1f", [1, 128], F32)
    o116 = sbp("o116", [1, 16], F32)
    a32 = sbp("a32", [128, 2, DS], F32)
    convw = sbp("convw", [128, 2, DCONV], F32)
    convb = sbp("convb", [128, 2, 1], F32)
    alog = sbp("alog", [128, 2, DS], F32)
    dvec = sbp("dvec", [128, 2, 1], F32)
    scal = sbp("scal", [128, 16, 8], F32)
    tiny = sbp("tiny", [1, 8, 8], F32)
    zerosb = sbp("zerosb", [128, TCH], BF16)

    pA_h = es.enter_context(nc.psum_tensor("pA", [128, 2, TCH], F32))
    pB_h = es.enter_context(nc.psum_tensor("pB", [128, 2, TCH], F32))
    pC_h = es.enter_context(nc.psum_tensor("pC", [128, 2, TCH], F32))
    pD_h = es.enter_context(nc.psum_tensor("pD", [128, TCH], F32))
    pE_h = es.enter_context(nc.psum_tensor("pE", [128, TCH], F32))
    pA, pB, pC, pD, pE = (pA_h.ap(), pB_h.ap(), pC_h.ap(), pD_h.ap(),
                          pE_h.ap())
    pCb = pC_h.bitcast(BF16).ap()

    for dst, src in [(identb, ident_in), (o1f, onesf_in),
                     (o116, ones16_in),
                     (identf, identf_in),
                     (convw[:, 0, :], convw_in[0:128, :]),
                     (convw[:, 1, :], convw_in[128:256, :]),
                     (convb[:, 0, :], convb_in[0:128, :]),
                     (convb[:, 1, :], convb_in[128:256, :]),
                     (alog[:, 0, :], alog_in[0:128, :]),
                     (alog[:, 1, :], alog_in[128:256, :]),
                     (dvec[:, 0, :], dvec_in[0:128, :]),
                     (dvec[:, 1, :], dvec_in[128:256, :])]:
        dma("sync", dst, src, "ldw")
    LDW_DONE = P.dall("ldw")


    gdiag = SS("gdiag")
    for j in range(2):
        for k in range(DCONV):
            add("gpsimd",
                lambda e, j=j, k=k: e.affine_select(
                    out=convdiag[:, j, k, :],
                    in_=convw[:, j, k:k + 1].broadcast_to((128, 128)),
                    pattern=[[1, 128]], compare_op=ALU.is_equal, fill=0.0,
                    base=0, channel_multiplier=-1),
                wait=LDW_DONE, inc=gdiag)
        add("gpsimd",
            lambda e, j=j: e.affine_select(
                out=ddiag[:, j, :],
                in_=dvec[:, j, 0:1].broadcast_to((128, 128)),
                pattern=[[1, 128]], compare_op=ALU.is_equal, fill=0.0,
                base=0, channel_multiplier=-1),
            wait=LDW_DONE, inc=gdiag)
    gdiag_done = gdiag.v

    sA = SS("sA")
    sA_t = add("scalar",
               lambda e: e.activation(a32[:, :, :], alog[:, :, :], AF.Exp,
                                      bias=0.0, scale=1.0),
               wait=LDW_DONE, inc=sA)
    vA = SS("vA")
    add("vector", lambda e: e.tensor_scalar_mul(a32[:, :, :], a32[:, :, :],
                                                -1.0),
        wait=[(sA, sA_t)], inc=vA)
    vA_done = vA.v

    gz = SS("gz")
    gz_t = add("gpsimd", lambda e: e.memset(zerosb[:, :], 0.0), inc=gz)

    xt_h = _FSB(nc, "xt", [128, 2, DM], F32)
    sqd_h = _FSB(nc, "sqd", [128, DM], F32)
    lnst_h = _FSB(nc, "lnst", [128, 8, 8], F32)
    xn32_h = _FSB(nc, "xn32", [128, 2, DM], F32)
    xnT32_h = _FSB(nc, "xnT32", [128, 8, TOK], F32)
    xnbst_h = _FSB(nc, "xnbst", [128, 8, 128], BF16)
    sxacc_h = _FSB(nc, "sxacc", [128, 2, 8], F32)
    kqi_h = _FSB(nc, "kqi", [128, TOK], I16)
    kqb_h = _FSB(nc, "kqb", [128, 2, TOK], BF16)
    xt = xt_h.ap()
    sqd = sqd_h.ap()
    lnst = lnst_h.ap()
    xn32 = xn32_h.ap()
    xnT32 = xnT32_h.ap()
    xnbst = xnbst_h.ap()
    sxacc = sxacc_h.ap()
    kqi = kqi_h.ap()
    kqb = kqb_h.ap()

    cc_agq = {}
    vln = SS("vln")
    sln = SS("sln")
    gxn_pairs = []
    tln = SS("tln")
    sevac = SS("sevac")
    vevac = SS("vevac")
    cc = SS("cc")

    VPT = 11
    for tt in range(NT0):
        bsl = tt % 2
        pldx = dma("sync", xt[:, bsl, :], x_tok[tt * 128:(tt + 1) * 128, :],
                   "ldx", wait=[(vln, max(0, (tt - 2) * VPT + 9)),
                                (sln, max(0, (tt - 2) * 2 + 1))], rot=2)
        add("scalar",
            lambda e, b=bsl: e.activation(sqd[:, :], xt[:, b, :], AF.Square),
            wait=[pldx, (vln, max(0, (tt - 1) * VPT + 5))] +
            (P.dall("dbg") if DEBUG else []), inc=sln)
        sln_sq = sln.v
        add("vector",
            lambda e: e.reduce_sum(lnst[:, 1, 0:1], sqd[:, :], AXX),
            wait=[(sln, sln_sq)], inc=vln)
        add("vector",
            lambda e, b=bsl: e.reduce_sum(lnst[:, 0, 0:1], xt[:, b, :], AXX),
            wait=[pldx], inc=vln)
        add("vector", lambda e: e.tensor_scalar_mul(lnst[:, 0, 0:1], lnst[:, 0, 0:1],
                                                    1.0 / DM), inc=vln)
        add("vector", lambda e: e.tensor_mul(lnst[:, 5, 0:1], lnst[:, 0, 0:1],
                                             lnst[:, 0, 0:1]), inc=vln)
        add("vector",
            lambda e: e.tensor_scalar(lnst[:, 2, 0:1], lnst[:, 1, 0:1], 1.0 / DM,
                                      None, ALU.mult),
            wait=[(sln, sln_sq)], inc=vln)
        add("vector",
            lambda e: e.scalar_tensor_tensor(lnst[:, 2, 0:1], lnst[:, 5, 0:1], -1.0,
                                             lnst[:, 2, 0:1], ALU.mult, ALU.add),
            inc=vln)
        add("vector",
            lambda e: e.tensor_scalar(lnst[:, 2, 0:1], lnst[:, 2, 0:1], 1.0, EPS,
                                      ALU.mult, ALU.add), inc=vln)
        add("vector", lambda e: e.reciprocal(lnst[:, 3, 0:1], lnst[:, 2, 0:1]),
            inc=vln)
        add("scalar", lambda e: e.activation(lnst[:, 4, 0:1], lnst[:, 3, 0:1],
                                             AF.Sqrt, bias=0.0, scale=1.0),
            wait=[(vln, tt * VPT + 8)], inc=sln)
        sln_r = sln.v
        add("vector",
            lambda e, b=bsl: e.tensor_scalar(xn32[:, b, :], xt[:, b, :],
                                             lnst[:, 0, 0:1], lnst[:, 4, 0:1],
                                             ALU.subtract, ALU.mult),
            wait=[(sln, sln_r), (tln, max(0, (tt - 2) * 8 + 8))], inc=vln)
        vln_xn = vln.v
        add("vector",
            lambda e, b=bsl: e.reduce_max(sxacc[:, 1, 0:1], xn32[:, b, :], AXX,
                                          apply_absolute_value=True), inc=vln)
        if tt == 0:
            add("vector", lambda e: e.tensor_copy(sxacc[:, 0, 0:1],
                                                  sxacc[:, 1, 0:1]), inc=vln)
        else:
            add("vector", lambda e: e.tensor_max(sxacc[:, 0, 0:1], sxacc[:, 0, 0:1],
                                                 sxacc[:, 1, 0:1]), inc=vln)
        assert vln.v == (tt + 1) * VPT
        if False:
            dma("gpsimd", d_lnst[tt], lnst[:, :, :], "dbg",
                wait=[(vln, vln.v), (sln, sln.v)])
        for dj in range(8):
            dst = pA if dj < 4 else pB
            sub = dj % 4
            add("tensor",
                lambda e, b=bsl, dj=dj, dst=dst, sub=sub: e.transpose(
                    dst[:, sub // 2, (sub % 2) * 128:(sub % 2 + 1) * 128],
                    xn32[:, b, dj * 128:(dj + 1) * 128], identf[:, :]),
                wait=[(vln, vln_xn), (sevac, max(0, tt * 2))] + LDW_DONE,
                inc=tln)
        tln_t = tln.v
        for half in range(2):
            src = pA if half == 0 else pB
            add("scalar",
                lambda e, tt=tt, h=half, src=src: e.activation(
                    xnT32[:, h * 4:(h + 1) * 4,
                          tt * 128:(tt + 1) * 128].rearrange(
                              "p (a c) t -> p a c t", a=2),
                    src[:, :, 0:256].rearrange("p a (c t) -> p a c t", t=128),
                    AF.Copy, bias=0.0, scale=1.0),
                wait=[(tln, tln_t), (vevac, max(0, tt - 1))], inc=sevac)
        sevac_t = sevac.v
        add("vector",
            lambda e, tt=tt: e.tensor_copy(
                xnbst[:, :, :], xnT32[:, :, tt * 128:(tt + 1) * 128]),
            wait=[(sevac, sevac_t)] + (gxn_pairs[tt - 1] if tt > 0 else []),
            inc=vevac)
        vevac_t = vevac.v
        prs = []
        for a8 in range(8):
            prs.append(dma(
                "gpsimd",
                ag_xn_in[tt // 4][a8 * 128:(a8 + 1) * 128,
                                  (tt % 4) * 128:(tt % 4 + 1) * 128],
                xnbst[:, a8, :], "gxn", wait=[(vevac, vevac_t)], rot=4))
        gxn_pairs.append(prs)
        if tt % 4 == 3:
            q = tt // 4
            add("gpsimd",
                lambda e, q=q: e.collective_compute(
                    "AllGather", ALU.bypass,
                    replica_groups=[list(range(NCORE))],
                    ins=[ag_xn_in[q]], outs=[ag_xn_out[q]]),
                wait=[p for t4 in range(q * 4, q * 4 + 4)
                      for p in gxn_pairs[t4]], inc=cc)
            cc_agq[q] = cc.v
    vln_all = vln.v
    sevac_all = sevac.v
    tln_all = tln.v


    t_sx = SS("t_sx")
    add("tensor", lambda e: e.transpose(pD[0:1, 0:128], sxacc[:, 0, 0:1],
                                        identf[:, :]),
        wait=[(vln, vln_all)], inc=t_sx)
    v_sx = SS("v_sx")
    add("vector", lambda e: e.memset(tiny[:, :, :], 0.0), inc=v_sx)
    add("vector", lambda e: e.reduce_max(tiny[0:1, 0, 0:1], pD[0:1, 0:128], AXX),
        wait=[(t_sx, t_sx.v)], inc=v_sx)
    if False:
        dma("gpsimd", d_sx[0, 0:8], tiny[0:1, 0, 0:8],
            "dbg", wait=[(v_sx, v_sx.v)])
        dma("gpsimd", d_sx[3, :], sxacc[:, 0, 0:1],
            "dbg", wait=[(vln, vln_all)])
    parsx = dma("gpsimd", ars_in[0][:, :], tiny[0:1, 0, 0:8], "gars",
                wait=[(v_sx, v_sx.v)])
    add("gpsimd",
        lambda e: e.collective_compute(
            "AllReduce", ALU.max, replica_groups=[list(range(NCORE))],
            ins=[ars_in[0]], outs=[ars_out[0]]),
        wait=[parsx], inc=cc)
    cc_sx = cc.v
    pldsx = dma("scalar", tiny[0:1, 4, 0:8], ars_out[0][:, :], "lars",
                wait=[(cc, cc_sx)])
    v_sx2 = SS("v_sx2")
    if False:
        dma("gpsimd", d_sx[1, 0:8], tiny[0:1, 4, 0:8],
            "dbg", wait=[pldsx])
    add("vector",
        lambda e: e.tensor_scalar(tiny[0:1, 0, 0:1], tiny[0:1, 4, 0:1], 1.0 / QMAX,
                                  1e-12, ALU.mult, ALU.max),
        wait=[pldsx] + P.dall("dbg"), inc=v_sx2)
    t_sxb = SS("t_sxb")
    add("tensor",
        lambda e: e.matmul(pD[:, 0:1], o1f[0:1, :], tiny[0:1, 0, 0:1],
                           start=True, stop=True),
        wait=[(v_sx2, v_sx2.v), (v_sx, v_sx.v)] + LDW_DONE, inc=t_sxb)
    v_sx3 = SS("v_sx3")
    add("vector", lambda e: e.tensor_copy(scal[:, 0, 0:1], pD[:, 0:1]),
        wait=[(t_sxb, t_sxb.v)], inc=v_sx3)
    add("vector", lambda e: e.reciprocal(scal[:, 1, 0:1], scal[:, 0, 0:1]),
        inc=v_sx3)
    v_sx_done = v_sx3.v
    cc_agxn = cc_agq[3]
    for k in range(8):
        dma("gpsimd", winxTb[:, k, :], winxT[k * 128:(k + 1) * 128, :], "ldg")
        dma("gpsimd", winzTb[:, k, :], winzT[k * 128:(k + 1) * 128, :], "ldg")
    for j in range(2):
        dma("gpsimd", wxTb[:, j, :], wxT[j * 128:(j + 1) * 128, :], "ldg")
        dma("gpsimd", woutTb[:, j, :], woutT[j * 128:(j + 1) * 128, :], "ldg")
    LDG_DONE = P.dall("ldg")

    if False:
        dma("gpsimd", d_sx[2, 0:64],
            scal[0:1, 0:8, 0:8], "dbg",
            wait=[(v_sx3, v_sx_done)])
    vqx = SS("vqx")
    gkq_pairs = []
    for k in range(8):
        add("vector",
            lambda e, k=k: e.tensor_scalar(kqi[:, :], xnT32[:, k, :],
                                           scal[:, 1, 0:1], None, ALU.mult),
            wait=[(v_sx3, v_sx_done), (sevac, sevac_all)], inc=vqx)
        add("vector",
            lambda e, k=k: e.tensor_copy(kqb[:, k % 2, :], kqi[:, :]),
            wait=([gkq_pairs[k - 2]] if k >= 2 else []), inc=vqx)
        gkq_pairs.append(dma("gpsimd", kq_dram[k], kqb[:, k % 2, :], "gkq",
                             wait=[(vqx, vqx.v)], rot=2))
    vqx_done = vqx.v
    if DEBUG:
        for k in range(8):
            dma("gpsimd", d_xn[k, :, :], xnT32[:, k, :], "dbg",
                wait=[(sevac, sevac_all)], rot=2)
            dma("gpsimd", d_kq[k, :, :], kq_dram[k, :, :], "dbg",
                wait=P.dall("gkq"), rot=2)

    dqrows_h = _FSB(nc, "dqrows", [128, 3 * DM], F32)
    dqbm_h = _FSB(nc, "dqbm", [128, 3, DM // 32], F32)
    dqs_h = _FSB(nc, "dqs", [128, 3, DM // 32], F32)
    dqrs_h = _FSB(nc, "dqrs", [128, 3, DM // 32], F32)
    dqq_h = _FSB(nc, "dqq", [128, 3 * DM], I16)
    dqb_h = _FSB(nc, "dqb", [128, 3 * DM], BF16)
    dnbm_h = _FSB(nc, "dnbm", [128, IFF // 32], F32)
    dnsv_h = _FSB(nc, "dnsv", [128, IFF // 32], F32)
    dnrv_h = _FSB(nc, "dnrv", [128, IFF // 32], F32)
    tpst_h = _FSB(nc, "tpst", [128, 2, 128], BF16)
    dqrows = dqrows_h.ap()
    dqbm = dqbm_h.ap()
    dqs = dqs_h.ap()
    dqrs = dqrs_h.ap()
    dqq = dqq_h.ap()
    dqb = dqb_h.ap()
    dnbm = dnbm_h.ap()
    dnsv = dnsv_h.ap()
    dnrv = dnrv_h.ap()
    tpst = tpst_h.ap()

    vdq = SS("vdq")
    tdq = SS("tdq")
    sdqe = SS("sdqe")

    def emit_dequant(which):
        if which == "down":
            nrt, kdim = 1, IFF
            bmall, sall, rall = dnbm[:, :], dnsv[:, :], dnrv[:, :]
            agdst = agd_in
        else:
            nrt, kdim = 3, DM
            bmall, sall, rall = dqbm[:, :, :], dqs[:, :, :], dqrs[:, :, :]
            agdst = agu_in if which == "up" else agg_in
        nb32 = kdim // 32
        lddq_w = P.dall("lddq")
        tdq_prev = tdq.v
        sdqe_prev = sdqe.v
        for r in range(nrt):
            rin = dqrows[:, r * kdim:(r + 1) * kdim]
            bsl = dqbm[:, r, :] if which != "down" else dnbm[:, :]
            add("vector",
                lambda e, rin=rin, bsl=bsl: e.tensor_reduce(
                    bsl, rin.rearrange("p (a b) -> p a b", b=32), AXX, ALU.max,
                    apply_absolute_value=True),
                wait=lddq_w, inc=vdq)
        add("vector",
            lambda e, sall=sall, bmall=bmall: e.tensor_scalar(
                sall, bmall, 1.0 / 7.0, 1e-12, ALU.mult, ALU.max), inc=vdq)
        add("vector", lambda e, rall=rall, sall=sall: e.reciprocal(rall, sall),
            inc=vdq)
        for r in range(nrt):
            rin = dqrows[:, r * kdim:(r + 1) * kdim]
            qsl = dqq[:, r * kdim:(r + 1) * kdim]
            bsl2 = dqb[:, r * kdim:(r + 1) * kdim]
            rsl = dqrs[:, r, :] if which != "down" else dnrv[:, :]
            ssl = dqs[:, r, :] if which != "down" else dnsv[:, :]
            add("vector",
                lambda e, qsl=qsl, rin=rin, rsl=rsl, nb32=nb32: e.tensor_tensor(
                    qsl.rearrange("p (a b) -> p a b", b=32),
                    rin.rearrange("p (a b) -> p a b", b=32),
                    rsl.rearrange("p (a b) -> p a b", b=1).broadcast_to((128, nb32, 32)),
                    ALU.mult),
                wait=[(tdq, tdq_prev), (sdqe, sdqe_prev)], inc=vdq)
            add("vector",
                lambda e, qsl=qsl: e.tensor_scalar(qsl, qsl, -8.0, 7.0,
                                                   ALU.max, ALU.min), inc=vdq)
            add("vector",
                lambda e, bsl2=bsl2, qsl=qsl, ssl=ssl, nb32=nb32:
                e.tensor_tensor(
                    bsl2.rearrange("p (a b) -> p a b", b=32),
                    qsl.rearrange("p (a b) -> p a b", b=32),
                    ssl.rearrange("p (a b) -> p a b", b=1).broadcast_to((128, nb32, 32)),
                    ALU.mult), inc=vdq)
        vdq_t = vdq.v
        for r in range(nrt):
            for ckt in range(kdim // 128):
                slot = (r * (kdim // 128) + ckt) % 2
                add("tensor",
                    lambda e, r=r, ckt=ckt, kdim=kdim: e.transpose(
                        pCb[:, 0, 0:128],
                        dqb[:, r * kdim + ckt * 128:r * kdim + (ckt + 1) * 128],
                        identb[:, :]),
                    wait=[(vdq, vdq_t), (sdqe, sdqe.v), (tln, tln_all)],
                    inc=tdq)
                gh = P.dma_hist.get("gwc", []) if hasattr(P, "dma_hist") else []
                add("scalar",
                    lambda e, slot=slot: e.activation(
                        tpst[:, slot, :], pCb[:, 0, 0:128], AF.Copy, bias=0.0,
                        scale=1.0),
                    wait=[(tdq, tdq.v)] + ([gh[-2]] if len(gh) >= 2 else []),
                    inc=sdqe)
                dma("gpsimd", agdst[ckt, :, r * 128:(r + 1) * 128],
                    tpst[:, slot, :], "gwc", wait=[(sdqe, sdqe.v)])

    for r in range(3):
        dma("sync", dqrows[:, r * DM:(r + 1) * DM],
            wup_rows[r * 128:(r + 1) * 128, :], "lddq")
    emit_dequant("up")
    for r in range(3):
        dma("sync", dqrows[:, r * DM:(r + 1) * DM],
            wgt_rows[r * 128:(r + 1) * 128, :], "lddq", wait=[(vdq, vdq.v)])
    emit_dequant("gate")
    dma("sync", dqrows[:, :], wdn_rows[:, :], "lddq", wait=[(vdq, vdq.v)])
    emit_dequant("down")

    for (ain, aout) in [(agu_in, agu_out), (agg_in, agg_out),
                        (agd_in, agd_out)]:
        add("gpsimd",
            lambda e, ain=ain, aout=aout: e.collective_compute(
                "AllGather", ALU.bypass, replica_groups=[list(range(NCORE))],
                ins=[ain], outs=[aout]),
            wait=P.dall("gwc"), inc=cc)
    cc_agw = cc.v
    vdq_all = vdq.v
    sdqe_all = sdqe.v
    tdq_all = tdq.v

    for c in range(4, 8):
        for nt in range(8):
            for q in range(4):
                dma("sync", rs_in[0][c, nt, :, q * TCH:(q + 1) * TCH],
                    zerosb[:, :], "zrs", wait=[(gz, gz_t)], rot=2)
    for c in range(0, 4):
        for nt in range(8):
            for q in range(4):
                dma("sync", rs_in[1][c, nt, :, q * TCH:(q + 1) * TCH],
                    zerosb[:, :], "zrs", wait=[(gz, gz_t)], rot=2)
    ZRS_DONE = P.dall("zrs")
    tpst_h.free()
    dnrv_h.free()
    dnsv_h.free()
    dnbm_h.free()
    dqb_h.free()
    dqq_h.free()
    dqrs_h.free()
    dqs_h.free()
    dqbm_h.free()
    dqrows_h.free()
    kqb_h.free()
    kqi_h.free()
    sxacc_h.free()
    xnbst_h.free()
    xnT32_h.free()
    xn32_h.free()
    lnst_h.free()
    sqd_h.free()
    xt_h.free()

    x_inner_h = _FSB(nc, "x_inner", [128, 2, T], BF16)
    rhsxn_h = _FSB(nc, "rhsxn", [128, 2, 8, TCH], BF16)
    xzpre_h = _FSB(nc, "xzpre", [128, 2, 2, TCH + 3], BF16)
    ssmst_h = _FSB(nc, "ssmst", [33, 2, TCH], F32)
    x_inner = x_inner_h.ap()
    rhsxn = rhsxn_h.ap()
    xzpre = xzpre_h.ap()
    ssmst = ssmst_h.ap()

    REUSE_W = (P.dall("dbg") if DEBUG else []) + [
        (vqx, vqx_done), (vdq, vdq_all), (sdqe, sdqe_all),
        (tdq, tdq_all), (vln, vln_all),
        (sln, P.sems["sln"].v), (tln, tln_all),
        (vevac, P.sems["vevac"].v)] + P.dall("gwc") + P.dall("gxn") + P.dall("gkq")

    tfr = SS("tfr")
    sfe = SS("sfe")
    vha = SS("vha")
    tcv = SS("tcv")
    ssi = SS("ssi")
    tsm = SS("tsm")
    vse = SS("vse")
    gsm_pairs = []

    for ch in range(NCH):
        par = ch % 2
        rch = ch // 4
        coff = (ch % 4) * TCH
        bat = ch // CHB
        chb = ch % CHB
        pldr = dma("sync", rhsxn[:, par, :, :],
                   ag_xn_out[ch % 4][rch].rearrange("(a p) t -> p a t",
                                                    p=128),
                   "ldr", wait=[(cc, cc_agq[ch % 4]),
                                (tfr, max(0, (ch - 1) * 16))] + REUSE_W,
                   rot=2)
        for ntl in range(2):
            for k in range(8):
                add("tensor",
                    lambda e, ntl=ntl, k=k, par=par: e.matmul(
                        pA[:, ntl, :],
                        winxTb[:, k, ntl * 128:(ntl + 1) * 128],
                        rhsxn[:, par, k, :], start=(k == 0), stop=(k == 7)),
                    wait=[pldr] + LDG_DONE +
                    [(sfe, max(0, ch * 2)), (sevac, sevac_all),
                     (t_sxb, t_sxb.v), (tdq, tdq_all)],
                    inc=tfr)
        tfr_t = tfr.v
        if chb == 0:
            add("vector",
                lambda e, par=par: e.memset(xzpre[:, par, :, 0:3], 0.0),
                wait=[(ssi, max(0, (ch - 2) * 2 + 2))] + REUSE_W, inc=vha)
        else:
            add("vector",
                lambda e, par=par: e.tensor_copy(
                    xzpre[:, par, :, 0:3], xzpre[:, 1 - par, :, TCH:TCH + 3]),
                wait=[(sfe, max(0, ch * 2)),
                      (ssi, max(0, (ch - 2) * 2 + 2))], inc=vha)
        vha_t = vha.v
        for j in range(2):
            add("scalar",
                lambda e, j=j, par=par: e.activation(
                    xzpre[:, par, j, 3:TCH + 3], pA[:, j, :], AF.Copy,
                    bias=0.0, scale=1.0),
                wait=[(tfr, tfr_t), (vha, vha_t),
                      (tcv, max(0, (ch - 1) * 8))], inc=sfe)
        sfe_t = sfe.v
        for j in range(2):
            for k in range(DCONV):
                add("tensor",
                    lambda e, j=j, k=k, par=par: e.matmul(
                        pB[:, j, :], convdiag[:, j, k, :],
                        xzpre[:, par, j, k:k + TCH],
                        start=(k == 0), stop=(k == DCONV - 1)),
                    wait=[(sfe, sfe_t), (gdiag, gdiag_done),
                          (ssi, max(0, ch * 2))],
                    inc=tcv)
        tcv_t = tcv.v
        for j in range(2):
            add("scalar",
                lambda e, j=j, ch=ch: e.activation(
                    x_inner[:, j, ch * TCH:(ch + 1) * TCH], pB[:, j, :],
                    AF.Silu, bias=convb[:, j, 0:1], scale=1.0),
                wait=[(tcv, tcv_t)], inc=ssi)
        ssi_t = ssi.v
        for j in range(2):
            add("tensor",
                lambda e, j=j, ch=ch: e.matmul(
                    pE[0:33, :], wxTb[:, j, :],
                    x_inner[:, j, ch * TCH:(ch + 1) * TCH],
                    start=(j == 0), stop=(j == 1)),
                wait=[(ssi, ssi_t), (vse, max(0, ch))],
                inc=tsm)
        tsm_t = tsm.v
        add("vector",
            lambda e, par=par: e.tensor_copy(ssmst[:, par, :], pE[0:33, :]),
            wait=[(tsm, tsm_t)] +
            ([gsm_pairs[ch - 2]] if ch >= 2 else []), inc=vse)
        vse_t = vse.v
        gsm_pairs.append(dma(
            "gpsimd", ar_ssm_in[bat][:, chb * TCH:(chb + 1) * TCH],
            ssmst[:, par, :], "gsm", wait=[(vse, vse_t)], rot=2))
        if chb == CHB - 1:
            add("gpsimd",
                lambda e, bat=bat: e.collective_compute(
                    "AllReduce", ALU.add, replica_groups=[list(range(NCORE))],
                    ins=[ar_ssm_in[bat]], outs=[ar_ssm_out[bat]]),
                wait=gsm_pairs[bat * CHB:(bat + 1) * CHB], inc=cc)
    cc_ssm = {0: cc_agw + 1, 1: cc_agw + 2}
    assert cc.v == cc_ssm[1]
    P1T = {"tfr": tfr.v, "sfe": sfe.v, "tcv": tcv.v, "ssi": ssi.v,
           "tsm": tsm.v, "vse": vse.v}
    REUSE_P1 = [(tcv, P1T["tcv"]), (vse, P1T["vse"]),
                (vha, vha.v)] + P.dall("gsm")

    ssmst_h.free()
    xzpre_h.free()
    dspr_h = _FSB(nc, "dspr", [1, 2, 512], F32)
    bcstB_h = _FSB(nc, "bcstB", [16, TCH], F32)
    bcstC_h = _FSB(nc, "bcstC", [16, TCH], F32)
    bcstD_h = _FSB(nc, "bcstD", [16, TCH], F32)
    wch_h = _FSB(nc, "wch", [16, 2, TCH], BF16)
    cch_h = _FSB(nc, "cch", [16, 2, TCH], BF16)
    dAb_h = _FSB(nc, "dAb", [128, 2, 2, TCH], F32)
    ubuf_h = _FSB(nc, "ubuf", [128, 2, TCH], BF16)
    hbuf_h = _FSB(nc, "hbuf", [128, 2, 2, TCH], F32)
    vbuf_h = _FSB(nc, "vbuf", [128, 2, 2, TCH], BF16)
    state_h = _FSB(nc, "state", [128, 2, 2, DS], F32)
    gsil_h = _FSB(nc, "gsil", [128, 2, TCH], BF16)
    abuf_h = _FSB(nc, "abuf", [128, 2, TCH], BF16)
    attnst_h = _FSB(nc, "attnst", [128, 4, TCH], BF16)
    bigB_h = _FSB(nc, "bigB", [128, 2, DS, TCH], BF16)
    bigC_h = _FSB(nc, "bigC", [128, 2, DS, TCH], BF16)
    bcw_dram = dint("bcw_dram", [2, DS, TCH], BF16)
    bcc_dram = dint("bcc_dram", [2, DS, TCH], BF16)
    dspr = dspr_h.ap()
    bcstB = bcstB_h.ap()
    bcstC = bcstC_h.ap()
    bcstD = bcstD_h.ap()
    wch = wch_h.ap()
    cch = cch_h.ap()
    dAb = dAb_h.ap()
    ubuf = ubuf_h.ap()
    hbuf = hbuf_h.ap()
    vbuf = vbuf_h.ap()
    state = state_h.ap()
    gsil = gsil_h.ap()
    abuf = abuf_h.ap()
    attnst = attnst_h.ap()
    bigB = bigB_h.ap()
    bigC = bigC_h.ap()

    vwd = SS("vwd")
    td1 = SS("td1")
    sda = SS("sda")
    vu = SS("vu")
    vsc = SS("vsc")
    vst = SS("vst")
    vv = SS("vv")
    tsr = SS("tsr")
    tz = SS("tz")
    ssz = SS("ssz")
    va = SS("va")
    two = SS("two")
    swo = SS("swo")
    grs_pairs = []
    stw_pairs = []
    stc_pairs = []
    bigB_pairs = []
    bigC_pairs = []

    P1_DONE = [(sfe, P1T["sfe"]), (ssi, P1T["ssi"]), (vse, P1T["vse"]),
               (tsm, P1T["tsm"])]

    ssp = SS("ssp")
    dsp_pairs = [[], []]
    NSEG = L // 512
    for bat in range(2):
        for sg in range(NSEG):
            buf = sg % 2
            iglob = bat * NSEG + sg
            prevout = (dsp_pairs[bat] + (dsp_pairs[bat - 1] if bat else []))
            allprev = dsp_pairs[0] + dsp_pairs[1] if bat else dsp_pairs[0]
            pin = dma("sync", dspr[0:1, buf, :],
                      ar_ssm_out[bat][32:33, sg * 512:(sg + 1) * 512],
                      "ldbc", wait=[(cc, cc_ssm[bat]),
                                    (ssp, max(0, (iglob - 1) * 2))] +
                      REUSE_P1 +
                      ([allprev[iglob - 2]] if iglob >= 2 else []),
                      rot=3)
            add("scalar",
                lambda e, buf=buf: e.activation(dspr[0:1, buf, :],
                                                dspr[0:1, buf, :], AF.Exp,
                                                bias=0.0, scale=1.0),
                wait=[pin],
                inc=ssp)
            add("scalar",
                lambda e, buf=buf: e.activation(dspr[0:1, buf, :],
                                                dspr[0:1, buf, :], AF.Ln,
                                                bias=1.0, scale=1.0),
                inc=ssp)
            dsp_pairs[bat].append(dma(
                "gpsimd", dsp_dram[bat][:, sg * 512:(sg + 1) * 512],
                dspr[0:1, buf, :], "gsm", wait=[(ssp, ssp.v)], rot=2))

    if DEBUG:
        for bat in range(2):
            dma("gpsimd", d_dsp[bat], dsp_dram[bat][0, :], "dbg",
                wait=[dsp_pairs[bat][-1], dsp_pairs[bat][-2]] +
                [p for p in dsp_pairs[bat]])
        dma("gpsimd", d_a32[:, :, :], a32[:, :, :], "dbg",
            wait=[(vA, vA_done)])
    prol = {}

    def emit_prologue_a(ch):
        par = ch % 2
        bat = ch // CHB
        chb = ch % CHB
        wbc = [(cc, cc_ssm[bat]), (vwd, max(0, ch * 2)),
               (td1, max(0, ch))] + REUSE_P1
        pbcB = dma("sync", bcstB[:, :],
                   ar_ssm_out[bat][0:16, chb * TCH:(chb + 1) * TCH], "ldbc",
                   wait=wbc, rot=3)
        pbcC = dma("sync", bcstC[:, :],
                   ar_ssm_out[bat][16:32, chb * TCH:(chb + 1) * TCH], "ldbc",
                   wait=wbc, rot=3)
        pbcD = dma("sync", bcstD[:, :],
                   dsp_dram[bat][:, chb * TCH:(chb + 1) * TCH].broadcast_to(
                       (16, TCH)), "ldbc",
                   wait=wbc + [dsp_pairs[bat][(chb * TCH) // 512]], rot=3)
        prol[ch] = {"pbcB": pbcB, "pbcC": pbcC, "pbcD": pbcD}

    def emit_prologue_b(ch):
        par = ch % 2
        d = prol[ch]
        add("vector",
            lambda e, par=par: e.tensor_tensor(wch[:, par, :],
                                               bcstB[:, :],
                                               bcstD[:, :], ALU.mult),
            wait=[d["pbcB"], d["pbcD"]] + P1_DONE +
            ([stw_pairs[ch - 2]] if ch >= 2 else []) + REUSE_W,
            inc=vwd)
        add("scalar",
            lambda e, par=par: e.activation(cch[:, par, :],
                                            bcstC[:, :], AF.Copy,
                                            bias=0.0, scale=1.0),
            wait=[d["pbcC"]] + ([stc_pairs[ch - 2]] if ch >= 2 else []),
            inc=vwd)
        d["vwd_t"] = vwd.v
        stw_pairs.append(dma(
            "gpsimd", bcw_dram[par], wch[:, par, :], "stg",
            wait=[(vwd, d["vwd_t"])] +
            ([bigB_pairs[ch - 2][3]] if ch >= 2 else []), rot=2))
        stc_pairs.append(dma(
            "gpsimd", bcc_dram[par], cch[:, par, :], "stg",
            wait=[(vwd, d["vwd_t"])] +
            ([bigC_pairs[ch - 2][3]] if ch >= 2 else []), rot=2))
        bigB_pairs.append([dma(
            "sync",
            bigB[:, par, q * 4:(q + 1) * 4, :].rearrange("p a b -> p (a b)"),
            bcw_dram[par:par + 1, q * 4:(q + 1) * 4, :].rearrange(
                "x a b -> x (a b)").broadcast_to((128, 4 * TCH)),
            "bcb", wait=[stw_pairs[-1],
                         (vu, max(0, (ch - 2) * DS + (q + 1) * 4))],
            rot=4) for q in range(4)])
        bigC_pairs.append([dma(
            "sync",
            bigC[:, par, q * 4:(q + 1) * 4, :].rearrange("p a b -> p (a b)"),
            bcc_dram[par:par + 1, q * 4:(q + 1) * 4, :].rearrange(
                "x a b -> x (a b)").broadcast_to((128, 4 * TCH)),
            "bcc", wait=[stc_pairs[-1],
                         (vv, max(0, (ch - 2) * DS + (q + 1) * 4))],
            rot=4) for q in range(4)])

    def make_tail(ch, tz_t, tsr_t, pbank):
        par = ch % 2
        bat = ch // CHB
        chb = ch % CHB
        st = {}
        jobs = {}

        def silu():
            for j in range(2):
                add("scalar",
                    lambda e, j=j: e.activation(gsil[:, j, :], pC[:, j, :],
                                                AF.Silu, bias=0.0, scale=1.0),
                    wait=[(tz, tz_t), (va, max(0, ch))], inc=ssz)
            st["ssz_t"] = ssz.v

        def abuf_f():
            add("vector",
                lambda e: e.tensor_tensor(abuf[:, :, :], pbank[:, :, :],
                                          gsil[:, :, :], ALU.mult),
                wait=[(ssz, st["ssz_t"]), (tsr, tsr_t)], inc=va)
            st["va_t"] = va.v

        def wout(ntl):
            def f():
                add("tensor",
                    lambda e, ntl=ntl: e.matmul(
                        pC[:, ntl % 2, :],
                        woutTb[:, 0, ntl * 128:(ntl + 1) * 128], abuf[:, 0, :],
                        start=True, stop=False),
                    wait=[(va, st["va_t"]), (swo, max(0, ch * 8 + ntl - 1))],
                    inc=two)
                add("tensor",
                    lambda e, ntl=ntl: e.matmul(
                        pC[:, ntl % 2, :],
                        woutTb[:, 1, ntl * 128:(ntl + 1) * 128], abuf[:, 1, :],
                        start=False, stop=True),
                    inc=two)
                st["two_%d" % ntl] = two.v
            return f

        def evac(ntl):
            def f():
                gi = ch * 2 + ntl // 4
                add("scalar",
                    lambda e, ntl=ntl: e.activation(
                        attnst[:, ntl % 4, :], pC[:, ntl % 2, :], AF.Copy,
                        bias=0.0, scale=1.0),
                    wait=[(two, st["two_%d" % ntl])] +
                    ([grs_pairs[gi - 1]] if gi >= 1 else []), inc=swo)
                if ntl % 4 == 3:
                    grs_pairs.append(dma(
                        "gpsimd",
                        rs_in[bat][ch // 4,
                                   (ntl // 4) * 4:(ntl // 4) * 4 + 4, :,
                                   (ch % 4) * TCH:(ch % 4) * TCH +
                                   TCH].rearrange("a p t -> p a t"),
                        attnst[:, :, :], "grs",
                        wait=[(swo, swo.v)] + ZRS_DONE, rot=2))
                    if ntl == 7 and chb == CHB - 1:
                        add("gpsimd",
                            lambda e, bat=bat: e.collective_compute(
                                "ReduceScatter", ALU.add,
                                replica_groups=[list(range(NCORE))],
                                ins=[rs_in[bat]], outs=[rs_out[bat]]),
                            wait=grs_pairs[bat * CHB * 2:
                                           (bat + 1) * CHB * 2], inc=cc)
            return f

        jobs[0] = [silu]
        jobs[1] = [abuf_f]
        for ntl in range(8):
            jobs.setdefault(2 + ntl, []).append(wout(ntl))
            jobs.setdefault(3 + ntl, []).append(evac(ntl))
        return jobs

    pending = {}
    for ch in range(NCH):
        par = ch % 2
        bat = ch // CHB
        chb = ch % CHB
        t0 = ch * TCH
        sidx0 = ch * DS
        pbank = pB if ch % 2 == 0 else pA
        if ch == 0:
            emit_prologue_a(0)
            emit_prologue_b(0)
        pldz = dma("sync", rhsxn[:, par, :, :],
                   ag_xn_out[ch % 4][ch // 4].rearrange(
                       "(a p) t -> p a t", p=128),
                   "ldz", wait=[(tfr, P1T["tfr"]),
                                (tz, max(0, (ch - 1) * 16))], rot=2)
        add("tensor",
            lambda e: e.matmul(pD[:, :], o1f[0:1, :],
                                        bcstD[0:1, :],
                                        start=True, stop=True),
            wait=[(vwd, prol[ch]["vwd_t"]), (sda, max(0, ch * 32)),
                  (v_sx3, v_sx_done)], inc=td1)
        td1_t = td1.v
        if ch + 1 < NCH:
            emit_prologue_a(ch + 1)
        for s in range(DS):
            slot = s % 2
            si = sidx0 + s
            for j in range(2):
                add("scalar",
                    lambda e, j=j, s=s, slot=slot: e.activation(
                        dAb[:, slot, j, :], pD[:, :], AF.Exp, bias=0.0,
                        scale=a32[:, j, s:s + 1]),
                    wait=[(td1, td1_t), (vA, vA_done),
                          (vsc, max(0, (si - 1) * 2))],
                    inc=sda)
            sda_t = sda.v
            add("vector",
                lambda e, t0=t0, par=par, s=s: e.tensor_tensor(
                    ubuf[:, :, :], x_inner[:, :, t0:t0 + TCH],
                    bigB[:, par, s:s + 1, :].broadcast_to((128, 2, TCH)),
                    ALU.mult),
                wait=[bigB_pairs[ch][s // 4], (ssi, P1T["ssi"])], inc=vu)
            vu_t = vu.v
            for j in range(2):
                if chb == 0:
                    add("vector",
                        lambda e, j=j, slot=slot: e.tensor_tensor_scan(
                            hbuf[:, slot, j, :], dAb[:, slot, j, :],
                            ubuf[:, j, :],
                            0.0, ALU.mult, ALU.add),
                        wait=[(sda, sda_t), (vu, vu_t),
                              (vv, max(0, si - 1))], inc=vsc)
                else:
                    add("vector",
                        lambda e, j=j, slot=slot, ch=ch, s=s:
                        e.tensor_tensor_scan(
                            hbuf[:, slot, j, :], dAb[:, slot, j, :],
                            ubuf[:, j, :],
                            state[:, ch % 2, j, s:s + 1], ALU.mult, ALU.add),
                        wait=[(sda, sda_t), (vu, vu_t),
                              (vv, max(0, si - 1))], inc=vsc)
            vsc_t = vsc.v
            add("vector",
                lambda e, ch=ch, s=s, slot=slot: e.tensor_copy(
                    state[:, (ch + 1) % 2, :, s:s + 1],
                    hbuf[:, slot, :, TCH - 1:TCH]),
                inc=vst)
            if s >= 2:
                tsr_need = ch * 34 + 2 * (s - 2) + 2
            else:
                tsr_need = max(0, (ch - 1) * 34 + 2 * (14 + s) + 2)
            add("gpsimd",
                lambda e, slot=slot, par=par, s=s: e.tensor_tensor(
                    vbuf[:, slot, :, :], hbuf[:, slot, :, :],
                    bigC[:, par, s:s + 1, :].broadcast_to((128, 2, TCH)),
                    ALU.mult),
                wait=[(vsc, vsc_t), bigC_pairs[ch][s // 4],
                      (tsr, tsr_need)], inc=vv)
            vv_t = vv.v
            for j in range(2):
                add("tensor",
                    lambda e, s=s, slot=slot, j=j, pbank=pbank: e.matmul(
                        pbank[:, j, :], identb[:, :], vbuf[:, slot, j, :],
                        start=(s == 0), stop=False),
                    wait=[(vv, vv_t)] +
                    ([(va, max(0, ch - 1))] if s == 0 and j == 0 else []),
                    inc=tsr)
            if s == 1 and ch + 1 < NCH:
                emit_prologue_b(ch + 1)
            for f in pending.get(s, []):
                f()
        for j in range(2):
            add("tensor",
                lambda e, j=j, t0=t0, pbank=pbank: e.matmul(
                    pbank[:, j, :], ddiag[:, j, :], x_inner[:, j, t0:t0 + TCH],
                    start=False, stop=(j == 1)),
                wait=[(gdiag, gdiag_done)], inc=tsr)
        tsr_t = tsr.v

        for j in range(2):
            for k in range(8):
                add("tensor",
                    lambda e, j=j, k=k, par=par: e.matmul(
                        pC[:, j, :], winzTb[:, k, j * 128:(j + 1) * 128],
                        rhsxn[:, par, k, :], start=(k == 0), stop=(k == 7)),
                    wait=[pldz, (ssz, max(0, ch * 2)),
                          (swo, max(0, ch * 8)), (tdq, tdq_all)],
                    inc=tz)
        tz_t = tz.v
        pending = make_tail(ch, tz_t, tsr_t, pbank)
    for sl in sorted(pending):
        for f in pending[sl]:
            f()
    cc_rs = {0: cc_ssm[1] + 1, 1: cc_ssm[1] + 2}
    assert cc.v == cc_rs[1]
    P23T = {"tsr": tsr.v, "vu": vu.v, "vv": vv.v, "vsc": vsc.v, "va": va.v,
            "swo": swo.v, "two": two.v, "tz": tz.v, "ssz": ssz.v,
            "sda": sda.v, "td1": td1.v,
            "vst": vst.v}

    if DEBUG:
        dbg2 = SS("dbg")
        for j in range(2):
            dma("gpsimd", d_xi[j, :, :], x_inner[:, j, :], "dbg",
                wait=[(ssi, P1T["ssi"])], rot=2)
        for b in range(2):
            dma("gpsimd", d_ssm[b, :, :], ar_ssm_out[b][:, :], "dbg",
                wait=[(cc, cc_ssm[1])], rot=2)
        for c in range(8):
            for nt in range(8):
                dma("gpsimd", d_rsin[c, nt, :, :],
                    rs_in[1][c, nt, :, :], "dbg",
                    wait=P.dall("grs") + [(cc, cc_rs[1])], rot=2)
        for b in range(2):
            for nt in range(8):
                dma("gpsimd", d_rsout[b, nt, :, :], rs_out[b][nt, :, :],
                    "dbg", wait=[(cc, cc_rs[1])], rot=2)
        for r in range(8):
            for kt in range(8):
                dma("gpsimd", d_agu[r, kt, :, :], agu_out[r, kt, :, :],
                    "dbg", wait=[(cc, cc_agw)], rot=2)

    bigC_h.free()
    bigB_h.free()
    attnst_h.free()
    abuf_h.free()
    gsil_h.free()
    state_h.free()
    vbuf_h.free()
    hbuf_h.free()
    ubuf_h.free()
    dAb_h.free()
    cch_h.free()
    wch_h.free()
    bcstD_h.free()
    bcstC_h.free()
    bcstB_h.free()
    dspr_h.free()
    rhsxn_h.free()
    x_inner_h.free()

    sptile_h = _FSB(nc, "sptile", [128, 2, TCH], F32)
    macc_h = _FSB(nc, "macc", [128, 8, 8], F32)
    redt_h = _FSB(nc, "redt", [128, 4, 8], F32)
    lhsA_h = _FSB(nc, "lhsA", [128, 2, 8, 128], BF16)
    lhsA = lhsA_h.ap()
    sptile = sptile_h.ap()
    macc = macc_h.ap()
    redt = redt_h.ap()

    P23_DONE = (P.dall("dbg") if DEBUG else []) + [
        (tsr, P23T["tsr"]), (swo, P23T["swo"]), (va, P23T["va"]),
                (vu, P23T["vu"]), (vv, P23T["vv"]), (vsc, P23T["vsc"]),
                (sda, P23T["sda"]), (ssz, P23T["ssz"]),
                (vst, P23T["vst"])] + P.dall("grs") + P.dall("gsm")

    tff = SS("tff")
    sfv = SS("sfv")
    vfm = SS("vfm")
    gsp_pairs = []
    kqm_e_h = _FSB(nc, "kqm_e", [128, 8, TOK], BF16, side="left")
    kqm_e = kqm_e_h.ap()
    kqm = kqm_e
    PKQ_PAIRS = [dma("sync", kqm_e[:, k, :], kq_dram[k, :, :], "ldkq",
                     wait=P.dall("gkq") + P23_DONE, rot=4)
                 for k in range(8)]

    for wi, (agout, spill, acc_col) in enumerate(
            [(agg_out, gt_spill, 0), (agu_out, up_spill, 2)]):
        for nt in range(24):
            r, c0 = nt // 3, (nt % 3) * 128
            buf = nt % 2
            idx = wi * 24 + nt
            ldl_prs = []
            for kt in range(8):
                ldl_w = [(cc, cc_agw), (tff, max(0, (idx - 1) * 32))]
                if idx == 0 and kt == 0:
                    ldl_w = ldl_w + P23_DONE
                ldl_prs.append(dma(
                    "sync", lhsA[:, buf, kt, :], agout[r, kt, :, c0:c0 + 128],
                    "ldl", wait=ldl_w, rot=4))
            for m in range(4):
                pm = pA if m % 2 == 0 else pB
                mi = idx * 4 + m
                w = ldl_prs + PKQ_PAIRS + [(sfv, max(0, mi - 1))]
                if idx == 0 and m == 0:
                    w = w + P23_DONE
                for kt in range(8):
                    add("tensor",
                        lambda e, buf=buf, kt=kt, m=m, pm=pm: e.matmul(
                            pm[:, 0, :], lhsA[:, buf, kt, :],
                            kqm[:, kt, m * TCH:(m + 1) * TCH],
                            start=(kt == 0), stop=(kt == 7)),
                        wait=w, inc=tff)
                tff_t = tff.v
                add("scalar",
                    lambda e, m=m, pm=pm: e.activation(
                        sptile[:, m % 2, :], pm[:, 0, :], AF.Copy, bias=0.0,
                        scale=scal[:, 0, 0:1]),
                    wait=[(tff, tff_t)] +
                    ([gsp_pairs[mi - 2]] if mi >= 2 else []),
                    inc=sfv)
                sfv_t = sfv.v
                add("vector",
                    lambda e, m=m: e.reduce_max(redt[:, 0, 0:1],
                                                sptile[:, m % 2, :], AXX,
                                                apply_absolute_value=True),
                    wait=[(sfv, sfv_t)], inc=vfm)
                if nt == 0 and m == 0:
                    add("vector",
                        lambda e, acc_col=acc_col: e.tensor_copy(
                            macc[:, acc_col, 0:1], redt[:, 0, 0:1]),
                        inc=vfm)
                else:
                    add("vector",
                        lambda e, acc_col=acc_col: e.tensor_max(
                            macc[:, acc_col, 0:1],
                            macc[:, acc_col, 0:1], redt[:, 0, 0:1]),
                        inc=vfm)
                gsp_pairs.append(dma(
                    "gpsimd", spill[nt, :, m * TCH:(m + 1) * TCH],
                    sptile[:, m % 2, :], "gsp", wait=[(vfm, vfm.v)], rot=4))
    vfm_a = vfm.v
    GSP_DONE = P.dall("gsp")

    if DEBUG:
        dbg3 = SS("dbg")
        for nt in range(24):
            dma("gpsimd", d_up[nt, :, :], up_spill[nt, :, :], "dbg",
                wait=GSP_DONE, rot=2)
    t4 = SS("t4")
    v4 = SS("v4")
    add("tensor", lambda e: e.transpose(pD[0:1, 0:128], macc[:, 0, 0:1],
                                        identf[:, :]),
        wait=[(vfm, vfm_a), (sda, P23T["sda"])], inc=t4)
    add("vector", lambda e: e.memset(tiny[0:1, 0, 0:8], 0.0),
        wait=[(vfm, vfm_a)], inc=v4)
    add("vector", lambda e: e.reduce_max(tiny[0:1, 0, 0:1], pD[0:1, 0:128], AXX),
        wait=[(t4, t4.v)], inc=v4)
    add("tensor", lambda e: e.transpose(pD[0:1, 0:128], macc[:, 2, 0:1],
                                        identf[:, :]),
        wait=[(v4, v4.v)], inc=t4)
    add("vector", lambda e: e.reduce_max(tiny[0:1, 0, 1:2], pD[0:1, 0:128], AXX),
        wait=[(t4, t4.v)], inc=v4)
    pg4 = dma("gpsimd", ars_in[1][:, :], tiny[0:1, 0, 0:8], "gars",
              wait=[(v4, v4.v)])
    add("gpsimd",
        lambda e: e.collective_compute(
            "AllReduce", ALU.max, replica_groups=[list(range(NCORE))],
            ins=[ars_in[1]], outs=[ars_out[1]]),
        wait=[pg4], inc=cc)
    cc_s2 = cc.v
    pld4 = dma("sync", tiny[0:1, 4, 0:8], ars_out[1][:, :], "lars",
               wait=[(cc, cc_s2)])
    v5 = SS("v5")
    add("vector",
        lambda e: e.tensor_scalar(tiny[0:1, 0, 0:1], tiny[0:1, 4, 0:1], 1.0 / QMAX,
                                  1e-12, ALU.mult, ALU.max),
        wait=[pld4], inc=v5)
    add("vector",
        lambda e: e.tensor_scalar(tiny[0:1, 1, 0:1], tiny[0:1, 4, 1:2], 1.0 / QMAX,
                                  1e-12, ALU.mult, ALU.max), inc=v5)
    t5 = SS("t5")
    add("tensor", lambda e: e.matmul(pD[:, 0:2], o1f[0:1, :], tiny[0:1, 0:2, 0:1],
                                     start=True, stop=True),
        wait=[(v5, v5.v)], inc=t5)
    add("vector", lambda e: e.tensor_copy(scal[:, 2, 0:1], pD[:, 0:1]),
        wait=[(t5, t5.v)], inc=v5)
    add("vector", lambda e: e.tensor_copy(scal[:, 4, 0:1], pD[:, 1:2]), inc=v5)
    add("vector", lambda e: e.reciprocal(scal[:, 3, 0:1], scal[:, 2, 0:1]), inc=v5)
    add("vector", lambda e: e.reciprocal(scal[:, 5, 0:1], scal[:, 4, 0:1]), inc=v5)
    v5_done = v5.v

    lhsA_h.free()
    kqm_e_h.free()
    kgar_h = _FSB(nc, "kgar", [128, 24, TOK], I16)
    mt_h = _FSB(nc, "mt", [128, TCH], F32)
    kut_h = _FSB(nc, "kut", [128, 2, TCH], I16)
    kst_h = _FSB(nc, "kst", [128, TCH], U16)
    sigt_h = _FSB(nc, "sigt", [128, TCH], F32)
    gtile_h = _FSB(nc, "gtile", [128, 2, TCH], F32)
    kgar = kgar_h.ap()
    mt = mt_h.ap()
    kut = kut_h.ap()
    kst = kst_h.ap()
    sigt = sigt_h.ap()
    gtile = gtile_h.ap()
    vkg = SS("vkg")
    for nt in range(24):
        for m in range(4):
            buf = (nt * 4 + m) % 2
            pgt = dma("sync", gtile[:, buf, :],
                      gt_spill[nt, :, m * TCH:(m + 1) * TCH],
                      "ldsp", wait=GSP_DONE +
                      [(vkg, max(0, nt * 4 + m - 1))], rot=2)
            add("vector",
                lambda e, nt=nt, m=m, buf=buf: e.tensor_scalar(
                    kgar[:, nt, m * TCH:(m + 1) * TCH], gtile[:, buf, :],
                    scal[:, 3, 0:1], None, ALU.mult),
                wait=[pgt, (v5, v5_done)], inc=vkg)
    vkg_done = vkg.v

    gtile_h.free()
    ssg = SS("ssg")
    vme = SS("vme")
    for nt in range(24):
        for m in range(4):
            i = nt * 4 + m
            add("scalar",
                lambda e, nt=nt, m=m: e.activation(
                    sigt[:, :], kgar[:, nt, m * TCH:(m + 1) * TCH], AF.Sigmoid,
                    bias=0.0, scale=scal[:, 2, 0:1]),
                wait=[(vkg, vkg_done), (vme, max(0, (i - 1) * 4 + 2))],
                inc=ssg)
            ssg_t = ssg.v
            add("vector",
                lambda e: e.tensor_scalar(kst[:, :], sigt[:, :], 65536.0,
                                          None, ALU.mult),
                wait=[(ssg, ssg_t)], inc=vme)
            add("vector",
                lambda e, nt=nt, m=m: e.tensor_tensor(
                    mt[:, :], kgar[:, nt, m * TCH:(m + 1) * TCH], kst[:, :],
                    ALU.mult), inc=vme)
            add("vector",
                lambda e: e.reduce_max(redt[:, 1, 0:1], mt[:, :], AXX,
                                       apply_absolute_value=True), inc=vme)
            if i == 0:
                add("vector", lambda e: e.tensor_copy(macc[:, 4, 0:1],
                                                      redt[:, 1, 0:1]), inc=vme)
            else:
                add("vector", lambda e: e.tensor_max(macc[:, 4, 0:1],
                                                     macc[:, 4, 0:1],
                                                     redt[:, 1, 0:1]), inc=vme)
    vme_done = vme.v
    t6 = SS("t6")
    add("tensor", lambda e: e.transpose(pD[0:1, 0:128], macc[:, 4, 0:1],
                                        identf[:, :]),
        wait=[(vme, vme_done)], inc=t6)
    v6 = SS("v6")
    add("vector", lambda e: e.memset(tiny[0:1, 0, 0:8], 0.0),
        wait=[(vme, vme_done)], inc=v6)
    add("vector", lambda e: e.reduce_max(tiny[0:1, 0, 0:1], pD[0:1, 0:128], AXX),
        wait=[(t6, t6.v)], inc=v6)
    pg5 = dma("gpsimd", ars_in[2][:, :], tiny[0:1, 0, 0:8], "gars",
              wait=[(v6, v6.v)])
    add("gpsimd",
        lambda e: e.collective_compute(
            "AllReduce", ALU.max, replica_groups=[list(range(NCORE))],
            ins=[ars_in[2]], outs=[ars_out[2]]),
        wait=[pg5], inc=cc)
    cc_s3 = cc.v
    pld5 = dma("sync", tiny[0:1, 4, 0:8], ars_out[2][:, :], "lars",
               wait=[(cc, cc_s3)])
    add("vector",
        lambda e: e.scalar_tensor_tensor(tiny[0:1, 0, 0:1], tiny[0:1, 4, 0:1],
                                         1.0 / (65536.0 * QMAX),
                                         scal[0:1, 2, 0:1], ALU.mult, ALU.mult),
        wait=[pld5], inc=v6)
    add("vector",
        lambda e: e.tensor_scalar(tiny[0:1, 1, 0:1], tiny[0:1, 0, 0:1], 1.0, 1e-12,
                                  ALU.mult, ALU.max), inc=v6)
    add("vector", lambda e: e.reciprocal(tiny[0:1, 2, 0:1], tiny[0:1, 1, 0:1]),
        inc=v6)
    add("vector",
        lambda e: e.scalar_tensor_tensor(tiny[0:1, 2, 0:1], tiny[0:1, 2, 0:1],
                                         1.0 / 65536.0, scal[0:1, 2, 0:1],
                                         ALU.mult, ALU.mult), inc=v6)
    add("tensor", lambda e: e.matmul(pD[:, 0:3], o1f[0:1, :], tiny[0:1, 0:3, 0:1],
                                     start=True, stop=True),
        wait=[(v6, v6.v)], inc=t6)
    add("vector", lambda e: e.tensor_copy(scal[:, 6, 0:1], pD[:, 2:3]),
        wait=[(t6, t6.v)], inc=v6)
    add("vector", lambda e: e.tensor_copy(scal[:, 7, 0:1], pD[:, 1:2]), inc=v6)
    v6_done = v6.v

    vg1 = SS("vg1")
    sg1 = SS("sg1")
    VG1 = 7
    for nt in range(24):
        for m in range(4):
            i = nt * 4 + m
            buf = i % 2
            add("scalar",
                lambda e, nt=nt, m=m: e.activation(
                    sigt[:, :], kgar[:, nt, m * TCH:(m + 1) * TCH], AF.Sigmoid,
                    bias=0.0, scale=scal[:, 2, 0:1]),
                wait=[(v6, v6_done), (vg1, max(0, (i - 1) * VG1 + 4))],
                inc=sg1)
            sg1_t = sg1.v
            add("vector",
                lambda e: e.tensor_scalar(kst[:, :], sigt[:, :], 65536.0,
                                          None, ALU.mult),
                wait=[(sg1, sg1_t)], inc=vg1)
            add("vector",
                lambda e, nt=nt, m=m: e.tensor_tensor(
                    mt[:, :], kgar[:, nt, m * TCH:(m + 1) * TCH], kst[:, :],
                    ALU.mult), inc=vg1)
            add("vector",
                lambda e, nt=nt, m=m: e.tensor_scalar(
                    kgar[:, nt, m * TCH:(m + 1) * TCH], mt[:, :],
                    scal[:, 6, 0:1], None, ALU.mult), inc=vg1)
            pldu = dma("sync", sptile[:, buf, :],
                       up_spill[nt, :, m * TCH:(m + 1) * TCH],
                       "ldsp", wait=[(vg1, max(0, (i - 1) * VG1 + 5))] +
                       GSP_DONE, rot=2)
            add("vector",
                lambda e, buf=buf: e.tensor_scalar(
                    kut[:, 0, :], sptile[:, buf, :], scal[:, 5, 0:1], None,
                    ALU.mult),
                wait=[pldu], inc=vg1)
            add("vector",
                lambda e, nt=nt, m=m: e.tensor_tensor(
                    mt[:, :], kgar[:, nt, m * TCH:(m + 1) * TCH],
                    kut[:, 0, :], ALU.mult), inc=vg1)
            add("vector",
                lambda e: e.reduce_max(redt[:, 2, 0:1], mt[:, :], AXX,
                                       apply_absolute_value=True), inc=vg1)
            if i == 0:
                add("vector", lambda e: e.tensor_copy(macc[:, 5, 0:1],
                                                      redt[:, 2, 0:1]), inc=vg1)
            else:
                add("vector", lambda e: e.tensor_max(macc[:, 5, 0:1],
                                                     macc[:, 5, 0:1],
                                                     redt[:, 2, 0:1]), inc=vg1)
            assert vg1.v == i * VG1 + VG1
    vg1_done = vg1.v
    t7 = SS("t7")
    add("tensor", lambda e: e.transpose(pD[0:1, 0:128], macc[:, 5, 0:1],
                                        identf[:, :]),
        wait=[(vg1, vg1_done)], inc=t7)
    v8 = SS("v8")
    add("vector", lambda e: e.memset(tiny[0:1, 0, 0:8], 0.0),
        wait=[(vg1, vg1_done)], inc=v8)
    add("vector", lambda e: e.reduce_max(tiny[0:1, 0, 0:1], pD[0:1, 0:128], AXX),
        wait=[(t7, t7.v)], inc=v8)
    pg6 = dma("gpsimd", ars_in[3][:, :], tiny[0:1, 0, 0:8], "gars",
              wait=[(v8, v8.v)])
    add("gpsimd",
        lambda e: e.collective_compute(
            "AllReduce", ALU.max, replica_groups=[list(range(NCORE))],
            ins=[ars_in[3]], outs=[ars_out[3]]),
        wait=[pg6], inc=cc)
    cc_s4 = cc.v
    pld6 = dma("sync", tiny[0:1, 4, 0:8], ars_out[3][:, :], "lars",
               wait=[(cc, cc_s4)])
    add("vector",
        lambda e: e.scalar_tensor_tensor(tiny[0:1, 0, 0:1], tiny[0:1, 4, 0:1],
                                         1.0 / QMAX, scal[0:1, 7, 0:1],
                                         ALU.mult, ALU.mult),
        wait=[pld6], inc=v8)
    add("vector",
        lambda e: e.scalar_tensor_tensor(tiny[0:1, 0, 0:1], tiny[0:1, 0, 0:1], 1.0,
                                         scal[0:1, 4, 0:1], ALU.mult, ALU.mult),
        inc=v8)
    add("vector",
        lambda e: e.tensor_scalar(tiny[0:1, 0, 0:1], tiny[0:1, 0, 0:1], 1.0, 1e-12,
                                  ALU.mult, ALU.max), inc=v8)
    add("vector", lambda e: e.reciprocal(tiny[0:1, 1, 0:1], tiny[0:1, 0, 0:1]),
        inc=v8)
    add("vector",
        lambda e: e.scalar_tensor_tensor(tiny[0:1, 1, 0:1], tiny[0:1, 1, 0:1], 1.0,
                                         scal[0:1, 7, 0:1], ALU.mult, ALU.mult),
        inc=v8)
    add("vector",
        lambda e: e.scalar_tensor_tensor(tiny[0:1, 1, 0:1], tiny[0:1, 1, 0:1], 1.0,
                                         scal[0:1, 4, 0:1], ALU.mult, ALU.mult),
        inc=v8)
    add("tensor", lambda e: e.matmul(pD[:, 0:2], o1f[0:1, :], tiny[0:1, 0:2, 0:1],
                                     start=True, stop=True),
        wait=[(v8, v8.v)], inc=t7)
    add("vector", lambda e: e.tensor_copy(scal[:, 8, 0:1], pD[:, 0:1]),
        wait=[(t7, t7.v)], inc=v8)
    add("vector", lambda e: e.tensor_copy(scal[:, 9, 0:1], pD[:, 1:2]), inc=v8)
    v8_done = v8.v

    if DEBUG:
        dma("gpsimd", d_scal[:, :], scal[:, :], "dbg",
            wait=[(v8, v8_done)])

    sigt_h.free()
    kst_h.free()
    qoar_h = _FSB(nc, "qoar", [128, 24, TCH], BF16)
    qoi_h = _FSB(nc, "qoi", [128, TCH], I16)
    ffm_h = _FSB(nc, "ffm", [128, 8, TCH], F32)
    lhsD_h = _FSB(nc, "lhsD", [128, 2, 24, 128], BF16)
    attt_h = _FSB(nc, "attt", [128, 2, 8, 128], BF16)
    sumt_h = _FSB(nc, "sumt", [128, 8, 128], F32)
    outt_h = _FSB(nc, "outt", [128, 2, DM], F32)
    xrt_h = _FSB(nc, "xrt", [128, 2, DM], F32)
    qoar = qoar_h.ap()
    qoi = qoi_h.ap()
    ffm = ffm_h.ap()
    lhsD = lhsD_h.ap()
    attt = attt_h.ap()
    sumt = sumt_h.ap()
    outt = outt_h.ap()
    xrt = xrt_h.ap()

    vg2 = SS("vg2")
    tdn = SS("tdn")
    sdn = SS("sdn")
    vfin = SS("vfin")
    tfin = SS("tfin")
    gout_pairs = []
    VG2 = 4

    for m in range(4):
        for kt in range(24):
            i = m * 24 + kt
            buf = i % 2
            pldu2 = dma("sync", sptile[:, buf, :],
                        up_spill[kt, :, m * TCH:(m + 1) * TCH],
                        "ldsp",
                        wait=[(vg2, max(0, (i - 1) * VG2 + 2)),
                              (vg1, vg1_done)], rot=2)
            add("vector",
                lambda e, buf=buf: e.tensor_scalar(
                    kut[:, 1, :], sptile[:, buf, :], scal[:, 5, 0:1], None,
                    ALU.mult),
                wait=[pldu2, (v8, v8_done)], inc=vg2)
            add("vector",
                lambda e, kt=kt, m=m: e.tensor_tensor(
                    mt[:, :], kgar[:, kt, m * TCH:(m + 1) * TCH], kut[:, 1, :],
                    ALU.mult),
                wait=[(tdn, max(0, m * 8 * 24))], inc=vg2)
            add("vector",
                lambda e: e.tensor_scalar(qoi[:, :], mt[:, :], scal[:, 9, 0:1],
                                          None, ALU.mult), inc=vg2)
            add("vector",
                lambda e, kt=kt: e.tensor_copy(qoar[:, kt, :], qoi[:, :]),
                inc=vg2)
            assert vg2.v == i * VG2 + VG2
        vg2_m = vg2.v
        for ntl in range(8):
            bufl = ntl % 2
            ldl2_prs = []
            for kt in range(24):
                ldl2_prs.append(dma(
                    "sync", lhsD[:, bufl, kt, :], agd_out[ntl, kt, :, :],
                    "ldl2",
                    wait=[(cc, cc_agw),
                          (tdn, max(0, (m * 8 + ntl - 1) * 24))], rot=4))
            for kt in range(24):
                add("tensor",
                    lambda e, bufl=bufl, kt=kt, ntl=ntl, m=m: e.matmul(
                        pE[:, :], lhsD[:, bufl, kt, :], qoar[:, kt, :],
                        start=(kt == 0), stop=(kt == 23)),
                    wait=ldl2_prs + [(vg2, vg2_m),
                          (sdn, max(0, m * 8 + ntl)),
                          (tsm, P1T["tsm"])],
                    inc=tdn)
            tdn_t = tdn.v
            add("scalar",
                lambda e, ntl=ntl: e.activation(ffm[:, ntl, :], pE[:, :],
                                                AF.Copy, bias=0.0,
                                                scale=scal[:, 8, 0:1]),
                wait=[(tdn, tdn_t), (vfin, max(0, m * 4 * 3))], inc=sdn)
        sdn_m = sdn.v
        for t4i in range(4):
            tt = m * 4 + t4i
            fbuf = tt % 2
            pats = []
            for b in range(2):
                pats.append(dma(
                    "sync", attt[:, b, :, :],
                    rs_out[b][:, :, tt * 128:(tt + 1) * 128].rearrange(
                        "a p t -> p a t"),
                    "ldfin", wait=[(cc, cc_rs[1]),
                                  (vfin, max(0, (tt - 1) * 3 + 1))], rot=2))
            pxr = dma("sync", xrt[:, fbuf, :],
                      x_tok[tt * 128:(tt + 1) * 128, :],
                      "ldfin", wait=[(vfin, max(0, (tt - 2) * 3 + 3))], rot=2)
            add("vector",
                lambda e: e.tensor_add(sumt[:, :, :], attt[:, 0, :, :],
                                       attt[:, 1, :, :]),
                wait=pats + [(sdn, sdn_m),
                      (tfin, max(0, (tt - 1) * 8))], inc=vfin)
            add("vector",
                lambda e, t4i=t4i: e.tensor_add(
                    sumt[:, :, :], sumt[:, :, :],
                    ffm[:, :, t4i * 128:(t4i + 1) * 128]), inc=vfin)
            vfin_t = vfin.v
            for a8 in range(8):
                add("tensor",
                    lambda e, a8=a8: e.transpose(
                        pC[:, a8 // 4, (a8 % 4) * 128:(a8 % 4 + 1) * 128],
                        sumt[:, a8, :], identf[:, :]),
                    wait=[(vfin, vfin_t), (swo, P23T["swo"]),
                          (vfin, max(0, (tt - 1) * 3 + 3))],
                    inc=tfin)
            tfin_t = tfin.v
            add("vector",
                lambda e, fbuf=fbuf: e.tensor_add(
                    outt[:, fbuf, :],
                    pC.rearrange("p a t -> p (a t)"), xrt[:, fbuf, :]),
                wait=[(tfin, tfin_t), pxr] +
                ([gout_pairs[tt - 2]] if tt >= 2 else []), inc=vfin)
            vfin_t2 = vfin.v
            gout_pairs.append(dma(
                "gpsimd", y_out[tt * 128:(tt + 1) * 128, :], outt[:, fbuf, :],
                "gout", wait=[(vfin, vfin_t2)], rot=2))

    add("gpsimd", lambda e: e.engine_nop(), wait=P.dall("gout"))

    with nc.Block() as block:
        @block.sync
        def _(eng):
            P.emit_engine("sync", eng)

        @block.gpsimd
        def _(eng):
            P.emit_engine("gpsimd", eng)

        @block.vector
        def _(eng):
            P.emit_engine("vector", eng)

        @block.scalar
        def _(eng):
            P.emit_engine("scalar", eng)

        @block.tensor
        def _(eng):
            P.emit_engine("tensor", eng)

    es.close()
    return nc


_CACHED = {}


def _get_nc():
    if "nc" not in _CACHED:
        info = {}
        build_program(collect=info)
        edges = compute_edges(info)
        _CACHED["nc"] = build_program(edges=edges)
    return _CACHED["nc"]


def kernel(L0_attn_x, L0_attn_W_in, L0_attn_conv_w, L0_attn_conv_b,
           L0_attn_W_x, L0_attn_A_log, L0_attn_D, L0_attn_W_out,
           L0_ffn_w_gate, L0_ffn_w_up, L0_ffn_w_down):
    x = np.ascontiguousarray(np.asarray(L0_attn_x, dtype=np.float32))
    W_in = np.ascontiguousarray(np.asarray(L0_attn_W_in, dtype=np.float32))
    conv_w = np.ascontiguousarray(np.asarray(L0_attn_conv_w, dtype=np.float32))
    conv_b = np.ascontiguousarray(np.asarray(L0_attn_conv_b, dtype=np.float32))
    W_x = np.ascontiguousarray(np.asarray(L0_attn_W_x, dtype=np.float32))
    A_log = np.ascontiguousarray(np.asarray(L0_attn_A_log, dtype=np.float32))
    D = np.ascontiguousarray(np.asarray(L0_attn_D, dtype=np.float32))
    W_out = np.ascontiguousarray(np.asarray(L0_attn_W_out, dtype=np.float32))
    w_gate = np.ascontiguousarray(np.asarray(L0_ffn_w_gate, dtype=np.float32))
    w_up = np.ascontiguousarray(np.asarray(L0_ffn_w_up, dtype=np.float32))
    w_down = np.ascontiguousarray(np.asarray(L0_ffn_w_down, dtype=np.float32))

    xf = x.reshape(T, DM)

    ident = np.eye(128, dtype=ml_dtypes.bfloat16)
    identf = np.eye(128, dtype=np.float32)
    onesf = np.ones((1, 128), dtype=np.float32)
    ones16 = np.ones((1, 16), dtype=np.float32)

    in_maps = []
    for c in range(NCORE):
        dsl = slice(c * DSL, (c + 1) * DSL)
        osl = slice(c * OSL, (c + 1) * OSL)
        im = {
            "x_tok": xf[c * TOK:(c + 1) * TOK, :],
            "winxT": np.ascontiguousarray(W_in[dsl, :].T),
            "winzT": np.ascontiguousarray(W_in[DI + c * DSL:DI + (c + 1) * DSL, :].T),
            "wxT": np.ascontiguousarray(W_x[:, dsl].T),
            "woutT": np.ascontiguousarray(W_out[:, dsl].T),
            "convw_in": conv_w[dsl, 0, :],
            "convb_in": conv_b[dsl].reshape(DSL, 1),
            "alog_in": A_log[dsl, :],
            "dvec_in": D[dsl].reshape(DSL, 1),
            "wup_rows": w_up[osl, :],
            "wgt_rows": w_gate[osl, :],
            "wdn_rows": w_down[c * 128:(c + 1) * 128, :],
            "ident_in": ident,
            "identf_in": identf,
            "onesf_in": onesf,
            "ones16_in": ones16,
        }
        in_maps.append(im)

    nc = _get_nc()
    res = run_bass_kernel_spmd(nc, in_maps, core_ids=list(range(NCORE)))
    out = np.concatenate([np.asarray(res.results[c]["y_out"])
                          for c in range(NCORE)], axis=0)
    return out.reshape(B, L, DM)
